# version 14
# baseline (speedup 1.0000x reference)
from collections import defaultdict

import numpy as np
import ml_dtypes

import concourse.bass as bass
import concourse.mybir as mybir
from concourse.bass_utils import run_bass_kernel_spmd

F32 = mybir.dt.float32
BF16 = mybir.dt.bfloat16
U8 = mybir.dt.uint8
AF = mybir.ActivationFunctionType
ALU = mybir.AluOpType
AX = mybir.AxisListType
BF = ml_dtypes.bfloat16

DM = 2048
DI = 4096
DS = 64
NH = 32
HD = 128
CH = 256
NKC = 16
T = 1024
TH = T + 3
NCH = 4
LNEPS = 1e-5
RMSEPS = 1.1920929e-07
MASKNEG = -1e30

P0, P1A, P1B, P2, P3, P4A, P4B, P5, P6, P7 = range(10)


def _pack(inputs):
    x = np.asarray(inputs["L0_attn_x"], np.float32)
    w_in = np.asarray(inputs["L0_attn_W_in"], np.float32)
    conv_w = np.asarray(inputs["L0_attn_conv_w"], np.float32)
    conv_b = np.asarray(inputs["L0_attn_conv_b"], np.float32)
    a_log = np.asarray(inputs["L0_attn_A_log"], np.float32)
    dt_bias = np.asarray(inputs["L0_attn_dt_bias"], np.float32)
    d_vec = np.asarray(inputs["L0_attn_D"], np.float32)
    w_out = np.asarray(inputs["L0_attn_W_out"], np.float32)
    gw = np.asarray(inputs["L0_ffn_gate_w"])
    gs = np.asarray(inputs["L0_ffn_gate_scale"], np.float32)
    uw = np.asarray(inputs["L0_ffn_up_w"])
    us = np.asarray(inputs["L0_ffn_up_scale"], np.float32)
    dw = np.asarray(inputs["L0_ffn_down_w"])
    dsc = np.asarray(inputs["L0_ffn_down_scale"], np.float32)

    def lowbyte(a):
        return np.ascontiguousarray(a.view(np.uint8).reshape(*a.shape, 4)[..., 0])

    tiles = np.ascontiguousarray(w_in.T).reshape(NKC, 128, 8352)

    def wtile(c0, c1):
        w = tiles[:, :, c0:c1]
        nt = (c1 - c0) // 128
        w = w.reshape(NKC, 128, nt, 128)
        return np.ascontiguousarray(w.transpose(2, 1, 0, 3))

    win_bc = wtile(DI, DI + 4224)
    win_dt = np.ascontiguousarray(tiles[:, :, 8320:8352].transpose(1, 0, 2))
    win_z = wtile(0, DI)
    woutt = np.ascontiguousarray(w_out.T).reshape(32, 128, DM)

    gq = np.ascontiguousarray(
        np.ascontiguousarray(lowbyte(gw).T).reshape(NKC, 128, 8192).transpose(1, 0, 2))
    uq = np.ascontiguousarray(
        np.ascontiguousarray(lowbyte(uw).T).reshape(NKC, 128, 8192).transpose(1, 0, 2))
    dq = np.ascontiguousarray(
        np.ascontiguousarray(lowbyte(dw).T).reshape(64, 128, DM).transpose(1, 0, 2))

    kk = np.arange(128)
    qq = np.arange(CH)
    mneg = np.zeros((128, 2, CH), np.float32)
    for j in range(2):
        mneg[:, j, :] = np.where(qq[None, :] >= (j * 128 + kk)[:, None], 0.0, MASKNEG)

    mneg = mneg.astype(BF)
    shared = dict(
        win_bc=win_bc, win_dt=win_dt, win_z=win_z, wout=woutt,
        wg=gq, wu=uq, wd=dq,
        gs_col=np.ascontiguousarray(gs.reshape(64, 128).T),
        us_col=np.ascontiguousarray(us.reshape(64, 128).T),
        dsc_row=np.ascontiguousarray(dsc.reshape(1, DM)),
        convw=np.ascontiguousarray(conv_w.reshape(33, 128, 4).transpose(1, 0, 2)),
        convb=np.ascontiguousarray(conv_b.reshape(33, 128).T),
        a_log=np.ascontiguousarray(a_log.reshape(NH, 1)),
        dt_bias=np.ascontiguousarray(dt_bias.reshape(NH, 1)),
        d_rep=np.ascontiguousarray(np.broadcast_to(d_vec.reshape(1, NH), (128, NH))),
        ident_bf=np.eye(128, dtype=BF),
        ident_f=np.eye(128, dtype=np.float32),
        ones_row_f=np.ones((1, 128), np.float32),
        ones_row_bf=np.ones((1, 128), dtype=BF),
        neg128_row=np.full((1, 128), -128.0, dtype=BF),
        ones_col_bf=np.ones((128, 1), dtype=BF),
        rms_ones=np.full((128, 1), 1.0 / DI, dtype=BF),
        mneg=mneg,
        sel4=np.ascontiguousarray(
            np.eye(4, dtype=np.float32)[:, :, None] * np.ones((1, 1, 64), np.float32)),
        sel2=np.ascontiguousarray(
            np.eye(2, dtype=np.float32)[:, :, None] * np.ones((1, 1, 128), np.float32)),
        sel2b=np.ascontiguousarray(
            (np.eye(2)[:, :, None] * np.ones((1, 1, 128))).astype(BF)),
        fconst=np.ascontiguousarray(np.broadcast_to(np.array(
            [LNEPS, RMSEPS, -1.0, 1.0, 1.0 / DM, -128.0], np.float32), (128, 6))),
    )

    xf = x.reshape(2 * 4096, DM)
    per_core = []
    for c in range(8):
        b, pos = c // 4, c % 4
        r0 = b * 4096 + pos * T
        xc = np.zeros((TH, DM), np.float32)
        xc[3:] = xf[r0:r0 + T]
        if pos > 0:
            xc[0:3] = xf[r0 - 3:r0]
        m2t = np.zeros((4, 4), np.float32)
        mbias = np.full((4, 1), MASKNEG, np.float32)
        for j in range(4):
            if j < pos:
                mbias[j, 0] = 0.0
                for l in range(j + 1, pos):
                    m2t[l, j] = 1.0
        per_core.append(dict(x=xc, m2t=m2t, mbias=mbias))
    return shared, per_core


class SbufPlan:
    def __init__(self, base, top):
        self.base, self.top = base, top
        self.items = []

    def add(self, name, nbytes, ph0, ph1):
        nbytes = (nbytes + 31) & ~31
        cands = sorted({self.base} | {o + b for (_, o, b, p0, p1) in self.items
                                      if not (p1 < ph0 or p0 > ph1)})
        for off in cands:
            ok = all(p1 < ph0 or p0 > ph1 or off + nbytes <= o or off >= o + b
                     for (_, o, b, p0, p1) in self.items)
            if ok:
                assert off + nbytes <= self.top, \
                    f"sbuf overflow: {name} {nbytes}B at {off} (top {self.top})"
                self.items.append((name, off, nbytes, ph0, ph1))
                return off
        raise AssertionError("unreachable")


class Prog:
    ENGSEM = dict(t="T", v="V", s="S", g="G")

    NQSEM = 6

    def __init__(self):
        self.ops = {e: [] for e in "tvsgq"}
        self.cnt = defaultdict(int)
        self.mark = defaultdict(int)
        self.qidx = {"g": 0, "q": 0}

    def op(self, eng, fn, waits=(), inc=None, incn=1, hard=()):
        own = self.ENGSEM.get(eng)
        rw = []
        for tk in list(waits) + list(hard):
            if tk is None:
                continue
            sem, val = tk
            if sem == own and tk not in hard:
                continue
            if self.mark[(eng, sem)] >= val:
                continue
            self.mark[(eng, sem)] = val
            rw.append((sem, val))
        tok = None
        if inc:
            self.cnt[inc] += incn
            tok = (inc, self.cnt[inc])
        self.ops[eng].append((tuple(rw), fn, inc, incn))
        return tok

    def dma(self, q, out, in_, waits=()):
        base = "GQ" if q == "g" else "SQ"
        sem = f"{base}{self.qidx[q] % self.NQSEM}"
        self.qidx[q] += 1
        self_wait = (sem, self.cnt[sem]) if self.cnt[sem] else None
        return self.op(q, lambda e, o=out, i=in_: e.dma_start(out=o, in_=i),
                       tuple(waits) + (self_wait,), inc=sem, incn=16)

    def mm(self, out, lhsT, rhs, start, stop, waits=(), inc=None):
        return self.op("t", lambda e, o=out, l=lhsT, r=rhs, a=start, b=stop:
                       e.matmul(o, l, r, start=a, stop=b), waits, inc=inc)

    def transp(self, out, in_, ident, waits=(), inc="T"):
        return self.op("t", lambda e, o=out, i=in_, d=ident: e.transpose(o, i, d),
                       waits, inc=inc)

    def act(self, out, in_, func, bias=0.0, scale=1.0, accum=None, waits=(),
            inc="S", hard=()):
        return self.op("s", lambda e, o=out, i=in_, f=func, b=bias, sc=scale, a=accum:
                       e.activation(o, i, f, bias=b, scale=sc, accum_out=a),
                       waits, inc=inc, hard=hard)

    def tt(self, eng, out, a, b, alu, waits=()):
        return self.op(eng, lambda e, o=out, x=a, y=b, u=alu:
                       e.tensor_tensor(o, x, y, u), waits, inc=self.ENGSEM[eng])

    def ts(self, eng, out, in0, s1, s2, op0, op1=ALU.bypass, waits=(), hard=()):
        return self.op(eng, lambda e, o=out, i=in0, a=s1, b=s2, u=op0, w=op1:
                       e.tensor_scalar(o, i, a, b, u, w), waits,
                       inc=self.ENGSEM[eng], hard=hard)

    def stt(self, eng, out, in0, scalar, in1, op0, op1, waits=(), hard=()):
        return self.op(eng, lambda e, o=out, i=in0, sc=scalar, j=in1, u=op0, w=op1:
                       e.scalar_tensor_tensor(o, i, sc, j, u, w),
                       waits, inc=self.ENGSEM[eng], hard=hard)

    def copy(self, eng, out, in_, waits=()):
        return self.op(eng, lambda e, o=out, i=in_: e.tensor_copy(o, i),
                       waits, inc=self.ENGSEM[eng])

    def rsum(self, out, in_, waits=()):
        return self.op("v", lambda e, o=out, i=in_: e.reduce_sum(o, i, AX.X),
                       waits, inc="V")

    def scan(self, out, d0, d1, init, op0, op1, waits=()):
        return self.op("v", lambda e, o=out, a=d0, b=d1, i=init, u=op0, w=op1:
                       e.tensor_tensor_scan(o, a, b, i, u, w), waits, inc="V")

    def recip(self, out, in_, waits=()):
        return self.op("v", lambda e, o=out, i=in_: e.reciprocal(o, i),
                       waits, inc="V")

    def wait_only(self, eng, waits):
        for tk in waits:
            if tk is None:
                continue
            sem, val = tk
            if self.mark[(eng, sem)] >= val:
                continue
            self.mark[(eng, sem)] = val
            self.ops[eng].append((((sem, val),), None, None, 0))


def build_nc(taps=(), trunc=None):
    tapset = {t[0] for t in taps}
    nc = bass.Bass("TRN2", target_bir_lowering=False, debug=False,
                   detect_race_conditions=False)

    din = {}
    for name, shape, dt in [
        ("x", [TH, DM], F32),
        ("win_bc", [33, 128, NKC, 128], F32), ("win_dt", [128, NKC, 32], F32),
        ("win_z", [32, 128, NKC, 128], F32), ("wout", [32, 128, DM], F32),
        ("wg", [128, NKC, 8192], U8), ("wu", [128, NKC, 8192], U8),
        ("wd", [128, 64, DM], U8),
        ("gs_col", [128, 64], F32), ("us_col", [128, 64], F32),
        ("dsc_row", [1, DM], F32),
        ("convw", [128, 33, 4], F32), ("convb", [128, 33], F32),
        ("a_log", [NH, 1], F32), ("dt_bias", [NH, 1], F32), ("d_rep", [128, NH], F32),
        ("ident_bf", [128, 128], BF16), ("ident_f", [128, 128], F32),
        ("ones_row_f", [1, 128], F32), ("ones_row_bf", [1, 128], BF16),
        ("neg128_row", [1, 128], BF16),
        ("ones_col_bf", [128, 1], BF16), ("rms_ones", [128, 1], BF16),
        ("mneg", [128, 2, CH], BF16), ("m2t", [4, 4], F32), ("mbias", [4, 1], F32),
        ("sel4", [4, 4, 64], F32), ("sel2", [2, 2, 128], F32),
        ("sel2b", [2, 2, 128], BF16), ("fconst", [128, 6], F32),
    ]:
        din[name] = nc.dram_tensor(name, shape, dt, kind="ExternalInput").ap()

    y_out = nc.dram_tensor("y_out", [T, DM], F32, kind="ExternalOutput").ap()
    tout = {}
    for tname, shape, dt in taps:
        tout[tname] = nc.dram_tensor("tap_" + tname, shape, dt,
                                     kind="ExternalOutput").ap()

    xn_stash = nc.dram_tensor("xn_stash", [128, NKC, TH], BF16).ap()
    acc_stash = nc.dram_tensor("acc_stash", [128, 8, DM], F32).ap()
    cc_in = nc.dram_tensor("cc_in", [65, 4096], F32).ap()
    cc_out = nc.dram_tensor("cc_out", [260, 4096], F32).ap()

    plan = SbufPlan((nc.sbuf_base + 31) & ~31, nc.sbuf_top)
    S = {}

    def alloc(name, shape, dt, ph0, ph1):
        per_part = int(np.prod(shape[1:])) * mybir.dt.size(dt)
        off = plan.add(name, per_part, ph0, ph1)
        S[name] = nc.alloc_sbuf_tensor_at(name, list(shape), dt, offset=off)

    specs = [
        ("fconst", [128, 6], F32, P0, P7),
        ("identb", [128, 128], BF16, P0, P7), ("identf", [128, 128], F32, P0, P7),
        ("onesrowf", [1, 128], F32, P0, P7), ("onesrowb", [1, 128], BF16, P0, P7),
        ("sel4", [4, 4, 64], F32, P0, P7), ("sel2", [2, 2, 128], F32, P0, P7),
        ("sel2b", [2, 2, 128], BF16, P0, P7),
        ("neg128", [1, 128], BF16, P0, P7),
        ("onescol", [128, 1], BF16, P0, P7), ("rmsones", [128, 1], BF16, P0, P7),
        ("gscol", [128, 64], F32, P0, P1B), ("uscol", [128, 64], F32, P0, P1B),
        ("drep", [128, NH], F32, P0, P7),
        ("alog", [NH, 1], F32, P0, P7), ("acol", [NH, 1], F32, P0, P7),
        ("dtb", [NH, 1], F32, P0, P7),
        ("m2t", [4, 4], F32, P0, P7), ("mbias", [4, 1], F32, P0, P7),
        ("ldcol", [NH, 1], F32, P4A, P4B),
        ("ldrow", [1, NH], F32, P4A, P4A),
        ("mneg", [128, 2, CH], BF16, P0, P4B),
        ("convw", [128, 33, 4], F32, P0, P3), ("convb", [128, 33], F32, P0, P3),
        ("dscrow", [1, DM], F32, P0, P1A),
        ("screp", [128, DM], BF16, P1A, P1B), ("nsd", [1, DM], BF16, P1A, P1B),
        ("rsxn", [1, T], BF16, P1A, P1B), ("rsh", [1, T], BF16, P1B, P1B),
        ("xhalo", [3, DM], F32, P0, P0),
        ("lnscr", [128, DM], F32, P0, P0), ("lnsq", [128, DM], F32, P0, P0),
        ("lncol", [128, 64], F32, P0, P0), ("xntm", [128, DM], BF16, P0, P0),
        ("XN", [128, NKC, TH], BF16, P0, P2),
        ("XN2", [128, NKC, TH], BF16, P5, P5),
        ("ACC", [128, 8, DM], F32, P0, P1B),
        ("ACC2", [128, 8, DM], F32, P6, P7),
        ("XBC", [128, 33, TH], BF16, P2, P6),
        ("wgu", [128, 2, NKC, 512], BF16, P1B, P1B),
        ("tmpg", [128, 4, T], BF16, P1B, P1B),
        ("hb", [128, 8, T], BF16, P1B, P1B),
        ("wd", [128, 8, DM], BF16, P1B, P1B),
        ("ring", [128, 8, NKC, 128], BF16, P2, P2),
        ("ringz", [128, 8, NKC, 128], BF16, P5, P5),
        ("dt", [NH, T], F32, P2, P4A),
        ("dtT", [128, 8, NH], F32, P4A, P4B),
        ("dA", [NH, T], F32, P4A, P4A),
        ("acs", [NH, T], F32, P4A, P4B),
        ("acsT", [128, 8, NH], F32, P4A, P4B),
        ("dcol", [128, 8, NH], F32, P4A, P4A),
        ("decrep", [64, NCH, NH], F32, P4A, P4B),
        ("decrow", [1, NCH, NH], F32, P4A, P4A),
        ("lastrow", [1, NCH, NH], F32, P4A, P4A),
        ("rrow", [2, 2, 2, 128], F32, P4B, P4B),
        ("exprow", [2, 2, 2, 128], BF16, P4B, P4B),
        ("lbf2", [128, 2, 512], BF16, P4B, P4B),
        ("rhs2", [128, 2, 2, 512], BF16, P4B, P4B),
        ("cs0", [128, T], F32, P2, P4A), ("cs1", [128, T], F32, P2, P4A),
        ("xd", [128, 2, 2, NH, HD], BF16, P4A, P4B),
        ("bt", [128, 2, 2, DS], BF16, P4A, P4A),
        ("bd", [128, 4, DS], BF16, P4A, P4A),
        ("states", [64, NCH, DI], BF16, P4A, P4B),
        ("carA", [64, DI], F32, P4A, P4B), ("carB", [64, DI], F32, P4A, P4B),
        ("sload", [64, 1024], F32, P4B, P4B),
        ("cbuf", [64, T], BF16, P3, P4B),
        ("ssb", [128, 2, CH], BF16, P4B, P4B),
        ("ltmp", [128, 2, 512], F32, P4B, P4B),

        ("cx", [64, 2, 2 * CH], BF16, P4B, P4B),
        ("prevbf", [64, 2, 2, HD], BF16, P4B, P4B),
        ("wgt", [4, NH], F32, P4B, P4B), ("wrep", [64, 4, NH], F32, P4B, P4B),
        ("ldg", [4, NH], F32, P4B, P4B),
        ("zrow", [1, 256], F32, P4A, P4A),
        ("ysq", [128, T], BF16, P4B, P4B),
        ("rmsrow", [1, T], F32, P4B, P5),
        ("szt", [128, T], BF16, P5, P5), ("tmpy", [128, T], BF16, P5, P5),
        ("wor0", [128, 8, DM], BF16, P5, P6),
        ("wor1", [128, 8, DM], BF16, P6, P6),
    ]
    specs.sort(key=lambda a: -int(np.prod(a[1][1:])) * mybir.dt.size(a[2]))
    for args in specs:
        alloc(*args)

    pb = [nc.alloc_psum_tensor(f"pb{i}", [128, 512], F32) for i in range(8)]

    def pbf(i):
        return pb[i].ap().bitcast(BF16)

    p = Prog()
    marks = {}

    def mark(name):
        marks[name] = {e: len(p.ops[e]) for e in p.ops}

    src_of = {"identb": "ident_bf", "identf": "ident_f", "onesrowf": "ones_row_f",
              "onesrowb": "ones_row_bf", "sel4": "sel4",
              "neg128": "neg128_row", "onescol": "ones_col_bf", "rmsones": "rms_ones",
              "gscol": "gs_col", "uscol": "us_col", "drep": "d_rep", "alog": "a_log",
              "dtb": "dt_bias", "m2t": "m2t", "mbias": "mbias", "mneg": "mneg",
              "sel2": "sel2", "sel2b": "sel2b",
              "convw": "convw", "convb": "convb", "dscrow": "dsc_row",
              "fconst": "fconst"}
    for n, srcn in src_of.items():
        consts_tok = p.dma("q", S[n][:], din[srcn][:])
    for i, v in enumerate([LNEPS, RMSEPS, -1.0, 1.0, 1.0 / DM, -128.0]):
        nc.const_aps.register(F32, v, S["fconst"][:, i:i + 1])

    xmain = din["x"][3:TH, :].rearrange("(tl q) d -> q tl d", q=128)
    tok_x = p.dma("q", S["ACC"][:], xmain)
    tok_xh = p.dma("q", S["xhalo"][:], din["x"][0:3, :])

    xn_evacs = []
    last_tp = None
    for t in range(9):
        if t < 8:
            xin, npt, w = S["ACC"][:, t, :], 128, (tok_x,)
        else:
            xin, npt, w = S["xhalo"][:], 3, (tok_xh,)
        col = S["lncol"]
        c0 = t * 6
        tv = p.rsum(col[0:npt, c0:c0 + 1], xin, waits=w)
        tv = p.act(col[0:npt, c0 + 1:c0 + 2], col[0:npt, c0:c0 + 1], AF.Copy,
                   scale=-1.0 / DM, waits=(tv,))
        ts1 = p.ts("v", S["lnscr"][0:npt, :], xin, col[0:npt, c0 + 1:c0 + 2],
                   None, ALU.add, waits=(tv,))
        ts2 = p.act(S["lnsq"][0:npt, :], S["lnscr"][0:npt, :], AF.Square,
                    accum=col[0:npt, c0 + 2:c0 + 3], waits=(ts1,))
        ts3 = p.act(col[0:npt, c0 + 3:c0 + 4], col[0:npt, c0 + 2:c0 + 3], AF.Sqrt,
                    scale=1.0 / DM, bias=LNEPS, waits=(ts2,))
        tv2 = p.recip(col[0:npt, c0 + 4:c0 + 5], col[0:npt, c0 + 3:c0 + 4],
                      waits=(ts3,))
        ts4 = p.act(S["xntm"][0:npt, :], S["lnscr"][0:npt, :], AF.Copy,
                    scale=col[0:npt, c0 + 4:c0 + 5],
                    waits=(tv2, consts_tok, last_tp))
        for kc in range(NKC):
            bank = 6 + (kc % 2)
            guard = xn_evacs[-2] if len(xn_evacs) >= 2 else None
            last_tp = p.transp(pbf(bank)[0:128, 0:npt],
                               S["xntm"][0:npt, bass.ts(kc, 128)],
                               S["identb"][0:npt, 0:npt], waits=(ts4, guard))
            dst = (S["XN"][:, kc, 3 + t * 128: 3 + (t + 1) * 128] if t < 8
                   else S["XN"][:, kc, 0:3])
            xn_evacs.append(p.act(dst, pbf(bank)[0:128, 0:npt], AF.Copy,
                                  waits=(last_tp,)))
    xn_tok = xn_evacs[-1]
    if "xn" in tapset:
        p.dma("q", tout["xn"][:], S["XN"][:], waits=(xn_tok,))

    mark("P0")
    sr_tok = None
    for n in range(4):
        tm = p.mm(pb[4][0:128, 0:512], S["onesrowf"][0:1, :],
                  S["dscrow"][0:1, bass.ts(n, 512)], True, True,
                  waits=(consts_tok, sr_tok, xn_tok), inc="T")
        sr_tok = p.copy("v", S["screp"][:, bass.ts(n, 512)], pb[4][0:128, 0:512],
                        waits=(tm,))
    tnsd = p.act(S["nsd"][:], S["dscrow"][:], AF.Copy, scale=-128.0,
                 waits=(consts_tok,))
    rsxn_tok = None
    for n in range(2):
        tm = None
        for kc in range(NKC):
            tm = p.mm(pb[6][0:1, 0:512], S["onescol"][:],
                      S["XN"][:, kc, 3 + n * 512: 3 + (n + 1) * 512],
                      kc == 0, kc == NKC - 1,
                      waits=(xn_tok, consts_tok, rsxn_tok) if kc == 0 else (),
                      inc="T" if kc == NKC - 1 else None)
        rsxn_tok = p.act(S["rsxn"][0:1, bass.ts(n, 512)], pb[6][0:1, 0:512],
                         AF.Copy, waits=(tm,))

    mark("P1A")
    ffn_last = None
    hrow_last = None
    wd_free = None
    hb_free = None
    wgu_last = [None, None]
    dn_prev = None
    wg_pref = None
    bank_guard = {0: None, 2: None}
    for pair in range(8):
        twd = p.dma("g", S["wd"][:], din["wd"][:, pair * 8:(pair + 1) * 8, :],
                    waits=(wd_free, sr_tok, xn_tok))
        wdsc = None
        for ic in range(8):
            wdsc = p.tt("v", S["wd"][:, ic, :], S["wd"][:, ic, :], S["screp"][:],
                        ALU.mult, waits=(twd,))
        gate_ev = {}
        h_tok = None
        up_prev_sb = None
        for sbi in range(2):
            for mat in range(2):
                slot = (sbi * 2 + mat) % 2
                wsrc = din["wg"] if mat == 0 else din["wu"]
                cols = slice((pair * 2 + sbi) * 512, (pair * 2 + sbi + 1) * 512)
                if pair > 0 and sbi == 0 and mat == 0:
                    wt = wg_pref
                else:
                    wt = p.dma("g", S["wgu"][:, slot, :, :], wsrc[:, :, cols],
                               waits=(wgu_last[slot], xn_tok))
                for m in range(4):
                    ic = sbi * 4 + m
                    gic = pair * 8 + ic
                    b0, b1 = (0, 1) if m % 2 == 0 else (2, 3)
                    first_w = [wt, xn_tok, dn_prev, up_prev_sb, bank_guard[b0]]
                    if mat == 1:
                        first_w += [gate_ev[(m, 0)], gate_ev[(m, 1)]]
                    for kc in range(NKC):
                        st = kc == 0
                        p.mm(pb[b0][0:128, 0:512],
                             S["wgu"][:, slot, kc, bass.ts(m, 128)],
                             S["XN"][:, kc, 3:515], st, False,
                             waits=tuple(first_w) if kc == 0 else ())
                        p.mm(pb[b1][0:128, 0:512],
                             S["wgu"][:, slot, kc, bass.ts(m, 128)],
                             S["XN"][:, kc, 515:1027], st, False)
                    p.mm(pb[b0][0:128, 0:512], S["neg128"][:],
                         S["rsxn"][0:1, 0:512], False, True, waits=(rsxn_tok,))
                    tm = p.mm(pb[b1][0:128, 0:512], S["neg128"][:],
                              S["rsxn"][0:1, 512:1024], False, True, inc="T")
                    wgu_last[slot] = tm
                    if mat == 0:
                        gate_ev[(m, 0)] = p.act(
                            S["tmpg"][:, m, 0:512], pb[b0][0:128, 0:512], AF.Silu,
                            scale=S["gscol"][:, gic:gic + 1],
                            waits=(tm, up_prev_sb))
                        gate_ev[(m, 1)] = p.act(
                            S["tmpg"][:, m, 512:1024], pb[b1][0:128, 0:512],
                            AF.Silu, scale=S["gscol"][:, gic:gic + 1], waits=(tm,))
                        bank_guard[b0] = gate_ev[(m, 1)]
                    else:
                        p.stt("v", S["hb"][:, ic, 0:512], pb[b0][0:128, 0:512],
                              S["uscol"][:, gic:gic + 1], S["tmpg"][:, m, 0:512],
                              ALU.mult, ALU.mult, waits=(tm, hb_free))
                        h_tok = p.stt("v", S["hb"][:, ic, 512:1024],
                                      pb[b1][0:128, 0:512],
                                      S["uscol"][:, gic:gic + 1],
                                      S["tmpg"][:, m, 512:1024],
                                      ALU.mult, ALU.mult, waits=(tm,))
                        bank_guard[b0] = h_tok
            up_prev_sb = h_tok
        for n in range(2):
            for ic in range(8):
                st = pair == 0 and ic == 0
                sp = pair == 7 and ic == 7
                tk = p.mm(pb[4 + n][0:1, 0:512], S["onescol"][:],
                          S["hb"][:, ic, bass.ts(n, 512)], st, sp,
                          waits=(h_tok,) if ic == 0 else (),
                          inc="T" if sp else None)
                if sp:
                    hrow_last = tk
        for t in range(8):
            b0, b1 = (0, 1) if t % 2 == 0 else (2, 3)
            for nb in range(2):
                tm = None
                for ic in range(8):
                    st = ic == 0
                    sp = ic == 7
                    p.mm(pb[b0][0:128, 0:512], S["hb"][:, ic, bass.ts(t, 128)],
                         S["wd"][:, ic, nb * 1024: nb * 1024 + 512], st, sp,
                         waits=(wdsc, h_tok, bank_guard[b0]) if ic == 0 else ())
                    tm = p.mm(pb[b1][0:128, 0:512], S["hb"][:, ic, bass.ts(t, 128)],
                              S["wd"][:, ic, nb * 1024 + 512:(nb + 1) * 1024],
                              st, sp, inc="T" if sp else None)
                p.tt("v", S["ACC"][:, t, nb * 1024: nb * 1024 + 512],
                     S["ACC"][:, t, nb * 1024: nb * 1024 + 512],
                     pb[b0][0:128, 0:512], ALU.add, waits=(tm, tok_x))
                ffn_last = p.tt("v", S["ACC"][:, t, nb * 1024 + 512:(nb + 1) * 1024],
                                S["ACC"][:, t, nb * 1024 + 512:(nb + 1) * 1024],
                                pb[b1][0:128, 0:512], ALU.add, waits=(tm,))
                bank_guard[b0] = ffn_last
        dn_prev = ffn_last
        wd_free = ffn_last
        hb_free = ffn_last
        if pair < 7:
            ncols = slice((pair + 1) * 2 * 512, ((pair + 1) * 2 + 1) * 512)
            wg_pref = p.dma("g", S["wgu"][:, 0, :, :], din["wg"][:, :, ncols],
                            waits=(wgu_last[0], xn_tok))

    e0 = p.act(S["rsh"][0:1, 0:512], pb[4][0:1, 0:512], AF.Copy, waits=(hrow_last,))
    e1 = p.act(S["rsh"][0:1, 512:1024], pb[5][0:1, 0:512], AF.Copy,
               waits=(hrow_last,))
    corr_prev = ffn_last
    acc_stash_tok = None
    acc_toks = []
    for t in range(8):
        for n in range(4):
            bank = n % 2
            tm = p.mm(pb[bank][0:128, 0:512], S["rsh"][0:1, bass.ts(t, 128)],
                      S["nsd"][0:1, bass.ts(n, 512)], True, True,
                      waits=(e0, e1, tnsd, corr_prev), inc="T")
            corr_prev = p.tt("v", S["ACC"][:, t, bass.ts(n, 512)],
                             S["ACC"][:, t, bass.ts(n, 512)],
                             pb[bank][0:128, 0:512], ALU.add, waits=(tm,))
        acc_stash_tok = p.dma("q", acc_stash[:, t, :], S["ACC"][:, t, :],
                              waits=(corr_prev,) +
                              (tuple(acc_toks) if t == 7 else ()))
        acc_toks.append(acc_stash_tok)
    ffn_tok = corr_prev
    if "ffn" in tapset:
        p.dma("q", tout["ffn"][:], S["ACC"][:], waits=(ffn_tok,))

    mark("P1B")
    ring_free = [None] * 8
    xbc_evac = []
    ev_pair = {0: None, 2: None}
    evh_prev = None
    dt_tok = None
    conv_done = []
    for mi in range(34):
        m = 33 if mi == 0 else mi - 1
        slot = mi % 8
        wsrc = din["win_bc"][m] if m < 33 else din["win_dt"]
        wdst = S["ring"][:, slot, :, :] if m < 33 else S["ring"][:, slot, :, 0:32]
        tw = p.dma("g", wdst, wsrc, waits=(ring_free[slot], acc_stash_tok))
        b0, b1 = (0, 1) if mi % 2 == 0 else (2, 3)
        mw = 128 if m < 33 else 32
        tm = None
        for kc in range(NKC):
            st, sp = kc == 0, kc == NKC - 1
            lhs = S["ring"][:, slot, kc, 0:mw]
            p.mm(pb[b0][0:mw, 0:512], lhs, S["XN"][:, kc, 3:515], st, sp,
                 waits=(tw, xn_tok, ev_pair[b0]) if kc == 0 else ())
            tm = p.mm(pb[b1][0:mw, 0:512], lhs, S["XN"][:, kc, 515:1027], st, sp,
                      inc="T" if sp else None)
        if m < 33:
            th = None
            for kc in range(NKC):
                th = p.mm(pb[6][0:mw, 0:3], S["ring"][:, slot, kc, 0:mw],
                          S["XN"][:, kc, 0:3], kc == 0, kc == NKC - 1,
                          waits=(evh_prev,) if kc == 0 else (),
                          inc="T" if kc == NKC - 1 else None)
            ring_free[slot] = th
            ev0 = p.act(S["XBC"][:, m, 3:515], pb[b0][0:128, 0:512], AF.Copy,
                        waits=(tm,))
            ev1 = p.act(S["XBC"][:, m, 515:1027], pb[b1][0:128, 0:512], AF.Copy,
                        waits=(tm,))
            evh_prev = p.act(S["XBC"][:, m, 0:3], pb[6][0:128, 0:3], AF.Copy,
                             waits=(th,))
            ev_pair[b0] = ev1
            xbc_evac.append((ev0, ev1, evh_prev))
            w = (ev0, ev1, evh_prev) + ((conv_done[-1],) if conv_done else ())
            t0c = p.ts("v", S["cs0"][:], S["XBC"][:, m, 0:T], S["convw"][:, m, 0:1],
                       S["convb"][:, m:m + 1], ALU.mult, ALU.add, waits=w)
            t1c = p.stt("v", S["cs1"][:], S["XBC"][:, m, 1:T + 1],
                        S["convw"][:, m, 1:2], S["cs0"][:], ALU.mult, ALU.add,
                        waits=(t0c,))
            t2c = p.stt("v", S["cs0"][:], S["XBC"][:, m, 2:T + 2],
                        S["convw"][:, m, 2:3], S["cs1"][:], ALU.mult, ALU.add,
                        waits=(t1c,))
            t3c = p.stt("v", S["cs1"][:], S["XBC"][:, m, 3:T + 3],
                        S["convw"][:, m, 3:4], S["cs0"][:], ALU.mult, ALU.add,
                        waits=(t2c,))
            conv_done.append(p.act(S["XBC"][:, m, 0:T], S["cs1"][:], AF.Silu,
                                   waits=(t3c,)))
        else:
            ring_free[slot] = tm
            last_a = None
            for half, bank in ((0, b0), (1, b1)):
                cols = slice(half * 512, (half + 1) * 512)
                r = p.act(S["cs0"][0:NH, cols], pb[bank][0:NH, 0:512], AF.Relu,
                          bias=S["dtb"][:], waits=(tm,))
                a = p.act(S["cs1"][0:NH, cols], pb[bank][0:NH, 0:512], AF.Abs,
                          bias=S["dtb"][:], waits=(tm,))
                last_a = a
                e = p.act(S["cs1"][0:NH, cols], S["cs1"][0:NH, cols], AF.Exp,
                          scale=-1.0, waits=(a,))
                l = p.act(S["cs1"][0:NH, cols], S["cs1"][0:NH, cols], AF.Ln,
                          bias=1.0, waits=(e,))
                dt_tok = p.tt("v", S["dt"][0:NH, cols], S["cs0"][0:NH, cols],
                              S["cs1"][0:NH, cols], ALU.add, waits=(l, r))
            ev_pair[b0] = last_a
    xn_stash_tok = p.dma("q", xn_stash[:], S["XN"][:], waits=(dt_tok, tm))
    tap_hold = None
    if "xbc" in tapset:
        tap_hold = p.dma("q", tout["xbc"][:], S["XBC"][:],
                         waits=(xbc_evac[-1][0], xbc_evac[-1][1]))

    conv_tok = conv_done[-1]
    if "xc" in tapset:
        p.dma("q", tout["xc"][:], S["XBC"][:, :, 0:T], waits=(conv_tok,))
    cb_tok = p.copy("v", S["cbuf"][:], S["XBC"][64:128, 32, 0:T], waits=(conv_tok,))

    mark("P3")
    tae = p.act(S["acol"][:], S["alog"][:], AF.Exp, waits=(consts_tok,))
    tac = p.act(S["acol"][:], S["acol"][:], AF.Copy, scale=-1.0, waits=(tae,))
    tda = p.ts("v", S["dA"][:], S["dt"][:], S["acol"][:], None, ALU.mult,
               waits=(tac, dt_tok, xn_stash_tok, acc_stash_tok))
    acs_tok = None
    for c in range(NCH):
        cols = slice(c * CH, (c + 1) * CH)
        acs_tok = p.scan(S["acs"][:, cols], S["dA"][:, cols], S["dA"][:, cols],
                         0.0, ALU.add, ALU.bypass, waits=(tda,))
    ld_tok = None
    tp_prev2 = [None, None]
    tp_toks = []
    for grp, srcbuf, dstbuf, wtok in ((0, "dt", "dtT", dt_tok),
                                      (1, "acs", "acsT", acs_tok)):
        for i in range(8):
            bank = 6 + i % 2
            tt = p.transp(pb[bank][0:128, 0:NH], S[srcbuf][:, bass.ts(i, 128)],
                          S["identf"][0:NH, 0:NH],
                          waits=(wtok, tp_prev2[i % 2], xn_stash_tok,
                                 acc_stash_tok))
            if dstbuf == "dtT":
                tp_prev2[i % 2] = p.copy("v", S[dstbuf][:, i, :],
                                         pb[bank][0:128, 0:NH], waits=(tt,))
            else:
                tp_prev2[i % 2] = p.act(S[dstbuf][:, i, :], pb[bank][0:128, 0:NH],
                                        AF.Copy, waits=(tt,))
            tp_toks.append(tp_prev2[i % 2])
    acst_tok = tp_toks[-1]
    dtT_tok = tp_toks[7]

    dcol_tok = None
    dec_tok = None
    lr_prev = None
    dr_prev = None
    last_ev_prev = acst_tok
    for c in range(NCH):
        tlt = p.transp(pb[3][0:1, 0:NH],
                       S["acs"][:, (c + 1) * CH - 1:(c + 1) * CH],
                       S["identf"][0:NH, 0:NH],
                       waits=(acs_tok, last_ev_prev, dr_prev))
        tle = p.act(S["lastrow"][0:1, c, :], pb[3][0:1, 0:NH], AF.Copy,
                    waits=(tlt,))
        last_ev_prev = tle
        tm = p.mm(pb[2][0:128, 0:NH], S["onesrowf"][0:1, :],
                  S["lastrow"][0:1, c, :], True, True,
                  waits=(tle, lr_prev), inc="T")
        for j in range(2):
            buf = "cs0" if j == 0 else "cs1"
            tv = p.tt("v", S[buf][0:128, 0:NH], pb[2][0:128, 0:NH],
                      S["acsT"][:, 2 * c + j, :], ALU.subtract,
                      waits=(tm, acst_tok))
            dcol_tok = p.act(S["dcol"][:, 2 * c + j, :], S[buf][0:128, 0:NH],
                             AF.Exp, waits=(tv,))
            lr_prev = tv
        trr = p.act(S["decrow"][0:1, c, :], S["lastrow"][0:1, c, :],
                    AF.Exp, waits=(tle,))
        tm2 = p.mm(pb[3][0:64, 0:NH], S["onesrowf"][0:1, 0:64],
                   S["decrow"][0:1, c, :], True, True,
                   waits=(trr, dr_prev, lr_prev), inc="T")
        dec_tok = p.act(S["decrep"][:, c, :], pb[3][0:64, 0:NH], AF.Copy,
                        waits=(tm2,))
        dr_prev = dec_tok

    ld_tok = p.tt("v", S["ldrow"][:], S["lastrow"][0:1, 0, :],
                  S["lastrow"][0:1, 1, :], ALU.add, waits=(last_ev_prev,))
    ld_tok = p.tt("v", S["ldrow"][:], S["ldrow"][:], S["lastrow"][0:1, 2, :],
                  ALU.add, waits=(ld_tok,))
    ld_tok = p.tt("v", S["ldrow"][:], S["ldrow"][:], S["lastrow"][0:1, 3, :],
                  ALU.add, waits=(ld_tok,))

    stash_fence = (xn_stash_tok, acc_stash_tok)
    xd_state = {"last_evac": acst_tok}

    def emit_xd(c, buf, extra=()):
        toks = []
        for j in range(2):
            for h in range(NH):
                bank = 6 + (h % 2)
                guard = toks[-2] if len(toks) >= 2 else xd_state["last_evac"]
                tt = p.transp(pbf(bank)[0:128, 0:128],
                              S["XBC"][:, h, c * CH + j * 128: c * CH + (j + 1) * 128],
                              S["identb"][:],
                              waits=(conv_tok, guard) + stash_fence + tuple(extra))
                toks.append(p.act(S["xd"][:, buf, j, h, :], pbf(bank)[0:128, 0:128],
                                  AF.Copy, scale=S["dtT"][:, 2 * c + j, h:h + 1],
                                  waits=(tt, dtT_tok)))
        xd_state["last_evac"] = toks[-1]
        return toks[-1]

    def emit_bt(c, buf):
        toks = []
        for j in range(2):
            bank = 6 + j
            tt = p.transp(pbf(bank)[0:128, 0:64],
                          S["XBC"][0:64, 32, c * CH + j * 128: c * CH + (j + 1) * 128],
                          S["identb"][0:64, 0:64],
                          waits=(conv_tok, xd_state["last_evac"],
                                 toks[-1] if toks else None))
            toks.append(p.act(S["bt"][:, buf, j, :], pbf(bank)[0:128, 0:64],
                              AF.Copy, waits=(tt,)))
        xd_state["last_evac"] = toks[-1]
        return toks[-1]

    mark("P4A_T")
    scan1_tok = None
    st_ev = None
    prev_tmm = None
    st_bank_rd = [dec_tok, dec_tok]
    bd_guard = [None] * 4
    for c in range(NCH):
        mark(f"P4A_C{c}")
        buf = c % 2
        xd_tok = emit_xd(c, buf)
        if c == 0:
            mark("P4A_XD")
            if "xd0" in tapset:
                tk = p.dma("q", tout["xd0"][:], S["xd"][:, 0, :, :, :],
                           waits=(xd_tok,))
                xd_state["last_evac"] = tk
        bt_tok = emit_bt(c, buf)
        if c == 0:
            mark("P4A_BT")
            if "bt0" in tapset:
                tk = p.dma("q", tout["bt0"][:], S["bt"][:, 0, :, :],
                           waits=(bt_tok,))
                xd_state["last_evac"] = tk
        for h in range(NH):
            if c == 0 and h in (1, 2, 16):
                mark(f"P4A_H{h}")
            bank = h % 2
            tmm = None
            for j in range(2):
                sl = (h % 2) * 2 + j
                tbd = p.ts("v", S["bd"][:, sl, :], S["bt"][:, buf, j, :],
                           S["dcol"][:, 2 * c + j, h:h + 1], None, ALU.mult,
                           waits=(bt_tok, dcol_tok, bd_guard[sl]))
                tmm = p.mm(pb[bank][0:64, 0:128], S["bd"][:, sl, :],
                           S["xd"][:, buf, j, h, :], j == 0, j == 1,
                           waits=(tbd, xd_tok, st_bank_rd[bank]), inc="T")
                bd_guard[sl] = tmm
            prev_tmm = tmm
            st_ev = p.copy("v", S["states"][:, c, h * HD:(h + 1) * HD],
                           pb[bank][0:64, 0:128], waits=(tmm,))
            st_bank_rd[bank] = st_ev
            if c == 0:
                scan1_tok = p.copy("v", S["carA"][:, h * HD:(h + 1) * HD],
                                   pb[bank][0:64, 0:128], waits=(tmm,))
                st_bank_rd[bank] = scan1_tok
        if c > 0:
            prevb = "carA" if c % 2 == 1 else "carB"
            nxtb = "carB" if c % 2 == 1 else "carA"
            for h in range(NH):
                scan1_tok = p.stt("v", S[nxtb][:, h * HD:(h + 1) * HD],
                                  S[prevb][:, h * HD:(h + 1) * HD],
                                  S["decrep"][:, c, h:h + 1],
                                  S["states"][:, c, h * HD:(h + 1) * HD],
                                  ALU.mult, ALU.add, waits=(st_ev, dec_tok))
    if "dt" in tapset:
        p.dma("q", tout["dt"][:], S["dt"][:], waits=(dt_tok,))
    if "ldc" in tapset:
        p.dma("q", tout["ldc"][:], S["ldrow"][:], waits=(ld_tok,))
    if "dcol" in tapset:
        p.dma("q", tout["dcol"][:], S["dcol"][:], waits=(dcol_tok,))
    if "lrow" in tapset:
        p.dma("q", tout["lrow"][:], S["lastrow"][:], waits=(dcol_tok,))
    if "acs" in tapset:
        p.dma("q", tout["acs"][:], S["acs"][:], waits=(acs_tok,))
    if "states" in tapset:
        p.dma("q", tout["states"][:], S["states"][:], waits=(scan1_tok,))
    if "score" in tapset:
        p.dma("q", tout["score"][:], S["carB"][:], waits=(scan1_tok,))
    mark("P4A_PRE")
    tz = p.op("v", lambda e, a=S["zrow"][:]: e.memset(a, 0.0),
              (xn_stash_tok, acc_stash_tok), inc="V")
    tz2 = p.copy("v", S["zrow"][0:1, 0:NH], S["ldrow"][:], waits=(tz, ld_tok))
    t1 = p.dma("q", cc_in[0:64, :], S["carB"][:], waits=(scan1_tok,))
    t2 = None
    for seg in range(16):
        t2 = p.dma("q", cc_in[64:65, 256 * seg:256 * (seg + 1)],
                   S["zrow"][:], waits=(tz2,))
    cc_tok = p.op("g", lambda g: g.collective_compute(
        "AllGather", ALU.bypass, replica_groups=[[0, 1, 2, 3], [4, 5, 6, 7]],
        ins=[cc_in.opt()], outs=[cc_out.opt()]), waits=(t1, t2), inc="G")

    mark("P4A")
    ldg_toks = [p.dma("q", S["ldg"][j:j + 1, :],
                      cc_out[65 * j + 64:65 * j + 65, 0:NH], waits=(cc_tok,))
                for j in range(4)]
    tm = p.mm(pb[2][0:4, 0:NH], S["m2t"][:], S["ldg"][:], True, True,
              waits=tuple(ldg_toks) + (consts_tok, lr_prev), inc="T")
    tw = p.act(S["wgt"][:], pb[2][0:4, 0:NH], AF.Exp, bias=S["mbias"][:],
               waits=(tm,))
    wr_tok = None
    for j in range(4):
        tm = p.mm(pb[3][0:64, 0:NH], S["sel4"][:, j, :], S["wgt"][:],
                  True, True, waits=(tw, dec_tok if j == 0 else wr_tok), inc="T")
        wr_tok = p.act(S["wrep"][:, j, :], pb[3][0:64, 0:NH], AF.Copy,
                       waits=(tm,))
    if "wgt0" in tapset:
        wr_tok = p.dma("q", tout["wgt0"][:], S["wgt"][:], waits=(tw, wr_tok))
        wr_tok = p.dma("q", tout["ldg0"][:], S["ldg"][:], waits=(wr_tok,))
        wr_tok = p.dma("q", tout["wrp0"][:], S["wrep"][:], waits=(wr_tok,))
    comb_tok = None
    prev_load = None
    for j in range(4):
        for half in range(4):
            tload = p.dma("q", S["sload"][:],
                          cc_out[65 * j:65 * j + 64, half * 1024:(half + 1) * 1024],
                          waits=(cc_tok, prev_load))
            for hh in range(NH // 4):
                h = half * (NH // 4) + hh
                if j == 0:
                    comb_tok = p.ts("v", S["carA"][:, h * HD:(h + 1) * HD],
                                    S["sload"][:, hh * HD:(hh + 1) * HD],
                                    S["wrep"][:, 0, h:h + 1], None, ALU.mult,
                                    waits=(tload, scan1_tok, wr_tok))
                else:
                    src = "carA" if j % 2 == 1 else "carB"
                    dst = "carB" if j % 2 == 1 else "carA"
                    comb_tok = p.stt("v", S[dst][:, h * HD:(h + 1) * HD],
                                     S["sload"][:, hh * HD:(hh + 1) * HD],
                                     S["wrep"][:, j, h:h + 1],
                                     S[src][:, h * HD:(h + 1) * HD],
                                     ALU.mult, ALU.add, waits=(tload, wr_tok))
            prev_load = comb_tok
    car = "carB"
    if "cin" in tapset:
        tap_cin = p.dma("q", tout["cin"][:], S["carB"][:], waits=(comb_tok,))
        comb_tok = tap_cin

    y_tok = None
    s_prev = None
    prev_ymm = [None, None]
    prev_tl = [lr_prev, lr_prev]
    prev_te = [None, None]
    prev_rhs = [None, None]
    prev_tcx = [wr_tok, wr_tok]
    pv_guard = [[None, None], [None, None]]
    prev_tce = [None, None]
    prev_tr = None
    rrow_guard = None
    y_bank_prev = [scan1_tok, scan1_tok]
    for c in range(NCH):
        buf = c % 2
        xd_tok = emit_xd(c, buf)
        rrow_guard = rrow_guard or xd_tok
        s_toks = []
        for j in range(2):
            tmS = p.mm(pb[j][0:128, 0:CH],
                       S["XBC"][0:64, 32, c * CH + j * 128: c * CH + (j + 1) * 128],
                       S["cbuf"][:, c * CH:(c + 1) * CH], True, True,
                       waits=(conv_tok, cb_tok, comb_tok, s_prev,
                              y_bank_prev[j]), inc="T")
            s_toks.append(p.act(S["ssb"][:, j, :], pb[j][0:128, 0:CH], AF.Copy,
                                waits=(tmS,)))
            s_prev = s_toks[-1]
            y_bank_prev[j] = s_toks[-1]
        for hp in range(NH // 2):
            par = hp % 2
            h0 = hp * 2
            rev = None
            for j in range(2):
                bank = 6 + j
                ttr = p.transp(pb[bank][0:2, 0:128],
                               S["acsT"][:, 2 * c + j, h0:h0 + 2], S["identf"][:],
                               waits=(acst_tok, rrow_guard, xd_tok))
                rev = p.act(S["rrow"][0:2, par, j, :], pb[bank][0:2, 0:128],
                            AF.Copy, waits=(ttr, prev_tr))
            rrow_guard = rev
            tex = p.act(S["exprow"][0:2, par, :, :], S["rrow"][0:2, par, :, :],
                        AF.Exp, waits=(rev, prev_tce[par]))
            p.mm(pb[2 + par][0:128, 0:CH], S["sel2"][:, 0, :],
                 S["rrow"][0:2, par, :, :].rearrange("p a b -> p (a b)"),
                 True, False, waits=(rev, prev_tl[par]))
            tr = p.mm(pb[2 + par][0:128, CH:2 * CH], S["sel2"][:, 1, :],
                      S["rrow"][0:2, par, :, :].rearrange("p a b -> p (a b)"),
                      False, True, inc="T")
            prev_tr = tr
            p.mm(pb[4 + par][0:64, 0:CH], S["sel2b"][:, 0, 0:64],
                 S["exprow"][0:2, par, :, :].rearrange("p a b -> p (a b)"),
                 True, False, waits=(tex, prev_tcx[par]))
            tce = p.mm(pb[4 + par][0:64, CH:2 * CH], S["sel2b"][:, 1, 0:64],
                       S["exprow"][0:2, par, :, :].rearrange("p a b -> p (a b)"),
                       False, True, inc="T")
            prev_tce[par] = tce
            tcx = p.tt("v", S["cx"][:, par, :].rearrange("p (a b) -> p a b", a=2),
                       S["cbuf"][:, c * CH:(c + 1) * CH].rearrange(
                           "p (a b) -> p a b", a=1).to_broadcast([64, 2, CH]),
                       pb[4 + par][0:64, 0:2 * CH].rearrange(
                           "p (a b) -> p a b", a=2),
                       ALU.mult, waits=(tce, cb_tok, prev_ymm[par]))
            prev_tcx[par] = tcx
            for j in range(2):
                tla = p.stt("v", S["ltmp"][:, par, 0:CH], pb[2 + par][0:128, 0:CH],
                            S["acsT"][:, 2 * c + j, h0:h0 + 1], S["mneg"][:, j, :],
                            ALU.subtract, ALU.add,
                            waits=(tr, acst_tok, consts_tok, prev_te[par]))
                tlb = p.stt("v", S["ltmp"][:, par, CH:2 * CH],
                            pb[2 + par][0:128, CH:2 * CH],
                            S["acsT"][:, 2 * c + j, h0 + 1:h0 + 2],
                            S["mneg"][:, j, :], ALU.subtract, ALU.add,
                            waits=(tla,))
                prev_tl[par] = tlb
                te = p.act(S["lbf2"][:, par, :], S["ltmp"][:, par, :], AF.Exp,
                           waits=(tlb, prev_rhs[par]))
                prev_te[par] = te
                trh = p.tt("v", S["rhs2"][:, par, j, :].rearrange(
                               "p (a b) -> p a b", a=2),
                           S["ssb"][:, j, :].rearrange(
                               "p (a b) -> p a b", a=1).to_broadcast([128, 2, CH]),
                           S["lbf2"][:, par, :].rearrange("p (a b) -> p a b", a=2),
                           ALU.mult, waits=(te, s_toks[j], prev_ymm[par]))
                prev_rhs[par] = trh
            for dh in range(2):
                h = h0 + dh
                tpv = p.copy("v", S["prevbf"][:, par, dh, :],
                             S[car][:, h * HD:(h + 1) * HD],
                             waits=(comb_tok, pv_guard[par][dh]))
                p.mm(pb[par][0:128, 0:CH], S["xd"][:, buf, 0, h, :],
                     S["rhs2"][:, par, 0, dh * CH:(dh + 1) * CH], True, False,
                     waits=(prev_rhs[par], xd_tok, y_bank_prev[par]))
                p.mm(pb[par][0:128, 0:CH], S["xd"][:, buf, 1, h, :],
                     S["rhs2"][:, par, 1, dh * CH:(dh + 1) * CH], False, False)
                tmy = p.mm(pb[par][0:128, 0:CH], S["prevbf"][:, par, dh, :],
                           S["cx"][:, par, dh * CH:(dh + 1) * CH],
                           False, True, waits=(tpv, tcx), inc="T")
                prev_ymm[par] = tmy
                pv_guard[par][dh] = tmy
                ysl = S["XBC"][:, h, c * CH:(c + 1) * CH]
                y_tok = p.stt("v", ysl, ysl, S["drep"][:, h:h + 1],
                              pb[par][0:128, 0:CH], ALU.mult, ALU.add,
                              waits=(tmy, consts_tok))
                y_bank_prev[par] = y_tok
        xd_state["last_evac"] = rrow_guard
        if c < NCH - 1:
            nxt = "carA" if car == "carB" else "carB"
            for h in range(NH):
                comb_tok = p.stt("v", S[nxt][:, h * HD:(h + 1) * HD],
                                 S[car][:, h * HD:(h + 1) * HD],
                                 S["decrep"][:, c, h:h + 1],
                                 S["states"][:, c, h * HD:(h + 1) * HD],
                                 ALU.mult, ALU.add, waits=(y_tok, dec_tok))
            car = nxt
    tap_hold2 = None
    if "yssd" in tapset:
        tap_hold2 = p.dma("q", tout["yssd"][:], S["XBC"][:, 0:32, 0:T],
                          waits=(y_tok,))

    rms_last = None
    tq = None
    for hc in range(NH):
        tq = p.tt("v", S["ysq"][:], S["XBC"][:, hc, 0:T], S["XBC"][:, hc, 0:T],
                  ALU.mult, waits=(y_tok, rms_last) if hc == 0 else (y_tok,))
        for n in range(2):
            st, sp = hc == 0, hc == NH - 1
            rms_last = p.mm(pb[2 + n][0:1, 0:512], S["rmsones"][:],
                            S["ysq"][:, bass.ts(n, 512)], st, sp,
                            waits=(tq,) + (tuple(prev_tl) if hc == 0 else ()),
                            inc="T" if sp else None)
    tr0 = p.act(S["rmsrow"][0:1, 0:512], pb[2][0:1, 0:512], AF.Sqrt, bias=RMSEPS,
                waits=(rms_last,))
    tr1 = p.act(S["rmsrow"][0:1, 512:1024], pb[3][0:1, 0:512], AF.Sqrt, bias=RMSEPS,
                waits=(rms_last,))
    trc = p.recip(S["rmsrow"][:], S["rmsrow"][:], waits=(tr0, tr1))
    rrep_tok = None
    for n in range(2):
        rrep_tok = p.mm(pb[4 + n][0:128, 0:512], S["onesrowf"][0:1, :],
                        S["rmsrow"][0:1, bass.ts(n, 512)], True, True,
                        waits=(trc,) + tuple(prev_tcx), inc="T")

    mark("P4B")
    fence5 = (rrep_tok, trc, tr1)
    xn2_tok = p.dma("q", S["XN2"][:], xn_stash[:],
                    waits=(xn_stash_tok,) + fence5)
    ringz_free = [None] * 8
    zev_pair = {0: (y_tok,), 2: (tr0, tr1)}
    ynz_tok = None
    for m in range(NH):
        slot = m % 8
        twz = p.dma("g", S["ringz"][:, slot, :, :], din["win_z"][m],
                    waits=(ringz_free[slot],) + fence5)
        b0, b1 = (0, 1) if m % 2 == 0 else (2, 3)
        guards = zev_pair[b0]
        tm = None
        for kc in range(NKC):
            st, sp = kc == 0, kc == NKC - 1
            p.mm(pb[b0][0:128, 0:512], S["ringz"][:, slot, kc, :],
                 S["XN2"][:, kc, 3:515], st, sp,
                 waits=(twz, xn2_tok) + guards if kc == 0 else ())
            tm = p.mm(pb[b1][0:128, 0:512], S["ringz"][:, slot, kc, :],
                      S["XN2"][:, kc, 515:1027], st, sp,
                      inc="T" if sp else None)
        ringz_free[slot] = tm
        tsz0 = p.act(S["szt"][:, 0:512], pb[b0][0:128, 0:512], AF.Silu, waits=(tm,))
        tsz1 = p.act(S["szt"][:, 512:1024], pb[b1][0:128, 0:512], AF.Silu,
                     waits=(tm,))
        pair_ev = None
        for n in range(2):
            tty = p.tt("v", S["tmpy"][:, bass.ts(n, 512)],
                       S["XBC"][:, m, n * 512:(n + 1) * 512],
                       pb[4 + n][0:128, 0:512], ALU.mult,
                       waits=(rrep_tok, y_tok, tsz0, tsz1, tap_hold2))
            ynz_tok = p.tt("v", S["XBC"][:, m, n * 512:(n + 1) * 512],
                           S["tmpy"][:, bass.ts(n, 512)],
                           S["szt"][:, bass.ts(n, 512)], ALU.mult, waits=(tty,))
            pair_ev = ynz_tok
        zev_pair[b0] = (pair_ev,)

    mark("P5")
    fence6 = (ynz_tok, ringz_free[7] or ringz_free[0], tsz1)
    acc2_tok = p.dma("q", S["ACC2"][:], acc_stash[:],
                     waits=(acc_stash_tok,) + fence6)
    wor_free = [None, None]
    WOR = (S["wor0"], S["wor1"])
    op_prev = {0: ynz_tok, 4: ynz_tok}
    out_tok = None
    for kb in range(4):
        slot = kb % 2
        twk = p.dma("g", WOR[slot][:, :, :],
                    din["wout"][kb * 8:(kb + 1) * 8].rearrange("k p d -> p k d"),
                    waits=(wor_free[slot],) + (fence5 if kb == 0 else fence6))
        for t in range(8):
            base = 0 if t % 2 == 0 else 4
            tm = None
            for kc in range(8):
                st, sp = kc == 0, kc == 7
                for n in range(4):
                    tm = p.mm(pb[base + n][0:128, 0:512],
                              S["XBC"][:, kb * 8 + kc, bass.ts(t, 128)],
                              WOR[slot][:, kc, bass.ts(n, 512)], st, sp,
                              waits=(twk, ynz_tok, op_prev[base])
                              if (kc == 0 and n == 0) else (),
                              inc="T" if (sp and n == 3) else None)
            evs = None
            for n in range(4):
                evs = p.tt("v", S["ACC2"][:, t, bass.ts(n, 512)],
                           S["ACC2"][:, t, bass.ts(n, 512)],
                           pb[base + n][0:128, 0:512], ALU.add,
                           waits=(tm, acc2_tok))
            op_prev[base] = evs
            out_tok = evs
        wor_free[slot] = out_tok

    mark("P6")
    ydst = y_out.rearrange("(tl q) d -> q tl d", q=128)
    st_tok = p.dma("q", ydst, S["ACC2"][:], waits=(out_tok,))
    p.wait_only("q", (st_tok,))

    mark("P7")
    if trunc is not None:
        cut = marks[trunc]
        for e in p.ops:
            p.ops[e] = p.ops[e][:cut[e]]

    from contextlib import ExitStack
    with ExitStack() as stack:
        sems = {}
        for nm in (["T", "V", "S", "G"]
                   + [f"GQ{i}" for i in range(Prog.NQSEM)]
                   + [f"SQ{i}" for i in range(Prog.NQSEM)]):
            sems[nm] = stack.enter_context(nc.semaphore("sem" + nm))
        blk = stack.enter_context(nc.Block())

        def runner(ops):
            def f(eng):
                for waits, fn, inc, incn in ops:
                    for (s, v) in waits:
                        eng.wait_ge(sems[s], v)
                    if fn is None:
                        continue
                    ins = fn(eng)
                    if inc:
                        ins.then_inc(sems[inc], incn)
            return f

        blk.tensor(runner(p.ops["t"]))
        blk.vector(runner(p.ops["v"]))
        blk.scalar(runner(p.ops["s"]))
        blk.gpsimd(runner(p.ops["g"]))
        blk.sync(runner(p.ops["q"]))

    return nc


_TAPS = ()


def kernel(**inputs):
    shared, per_core = _pack(inputs)
    nc = build_nc(_TAPS)
    in_maps = []
    for c in range(8):
        m = dict(shared)
        m.update(per_core[c])
        in_maps.append(m)
    res = run_bass_kernel_spmd(nc, in_maps, core_ids=list(range(8)))
    out = np.zeros((2, 4096, DM), np.float32)
    of = out.reshape(2 * 4096, DM)
    for c in range(8):
        b, pos = c // 4, c % 4
        r0 = b * 4096 + pos * T
        of[r0:r0 + T] = res.results[c]["y_out"]
    return out


# version 15 (final)
# speedup vs baseline: 1.1840x; 1.1840x over previous
from collections import defaultdict

import numpy as np
import ml_dtypes

import concourse.bass as bass
import concourse.mybir as mybir
from concourse.bass_utils import run_bass_kernel_spmd

F32 = mybir.dt.float32
BF16 = mybir.dt.bfloat16
U8 = mybir.dt.uint8
AF = mybir.ActivationFunctionType
ALU = mybir.AluOpType
AX = mybir.AxisListType
BF = ml_dtypes.bfloat16

DM = 2048
DI = 4096
DS = 64
NH = 32
HD = 128
CH = 256
NKC = 16
T = 1024
TH = T + 3
NCH = 4
LNEPS = 1e-5
RMSEPS = 1.1920929e-07
MASKNEG = -1e30

P0, P1A, P1B, P2, P3, P4A, P4B, P5, P6, P7 = range(10)


def _pack(inputs):
    x = np.asarray(inputs["L0_attn_x"], np.float32)
    w_in = np.asarray(inputs["L0_attn_W_in"], np.float32)
    conv_w = np.asarray(inputs["L0_attn_conv_w"], np.float32)
    conv_b = np.asarray(inputs["L0_attn_conv_b"], np.float32)
    a_log = np.asarray(inputs["L0_attn_A_log"], np.float32)
    dt_bias = np.asarray(inputs["L0_attn_dt_bias"], np.float32)
    d_vec = np.asarray(inputs["L0_attn_D"], np.float32)
    w_out = np.asarray(inputs["L0_attn_W_out"], np.float32)
    gw = np.asarray(inputs["L0_ffn_gate_w"])
    gs = np.asarray(inputs["L0_ffn_gate_scale"], np.float32)
    uw = np.asarray(inputs["L0_ffn_up_w"])
    us = np.asarray(inputs["L0_ffn_up_scale"], np.float32)
    dw = np.asarray(inputs["L0_ffn_down_w"])
    dsc = np.asarray(inputs["L0_ffn_down_scale"], np.float32)

    def lowbyte(a):
        return np.ascontiguousarray(a.view(np.uint8).reshape(*a.shape, 4)[..., 0])

    tiles = np.ascontiguousarray(w_in.T).reshape(NKC, 128, 8352)

    def wtile(c0, c1):
        w = tiles[:, :, c0:c1]
        nt = (c1 - c0) // 128
        w = w.reshape(NKC, 128, nt, 128)
        return np.ascontiguousarray(w.transpose(2, 1, 0, 3))

    win_bc = wtile(DI, DI + 4224)
    win_dt = np.ascontiguousarray(tiles[:, :, 8320:8352].transpose(1, 0, 2))
    win_z = wtile(0, DI)
    woutt = np.ascontiguousarray(w_out.T).reshape(32, 128, DM)

    gq = np.ascontiguousarray(
        np.ascontiguousarray(lowbyte(gw).T).reshape(NKC, 128, 8192).transpose(1, 0, 2))
    uq = np.ascontiguousarray(
        np.ascontiguousarray(lowbyte(uw).T).reshape(NKC, 128, 8192).transpose(1, 0, 2))
    dq = np.ascontiguousarray(
        np.ascontiguousarray(lowbyte(dw).T).reshape(64, 128, DM).transpose(1, 0, 2))

    kk = np.arange(128)
    qq = np.arange(CH)
    mneg = np.zeros((128, 2, CH), np.float32)
    for j in range(2):
        mneg[:, j, :] = np.where(qq[None, :] >= (j * 128 + kk)[:, None], 0.0, MASKNEG)

    mneg = mneg.astype(BF)
    shared = dict(
        win_bc=win_bc, win_dt=win_dt, win_z=win_z, wout=woutt,
        wg=gq, wu=uq, wd=dq,
        gs_col=np.ascontiguousarray(gs.reshape(64, 128).T),
        us_col=np.ascontiguousarray(us.reshape(64, 128).T),
        dsc_row=np.ascontiguousarray(dsc.reshape(1, DM)),
        convw=np.ascontiguousarray(conv_w.reshape(33, 128, 4).transpose(1, 0, 2)),
        convb=np.ascontiguousarray(conv_b.reshape(33, 128).T),
        a_log=np.ascontiguousarray(a_log.reshape(NH, 1)),
        dt_bias=np.ascontiguousarray(dt_bias.reshape(NH, 1)),
        d_rep=np.ascontiguousarray(np.broadcast_to(d_vec.reshape(1, NH), (128, NH))),
        ident_bf=np.eye(128, dtype=BF),
        ident_f=np.eye(128, dtype=np.float32),
        ones_row_f=np.ones((1, 128), np.float32),
        ones_row_bf=np.ones((1, 128), dtype=BF),
        neg128_row=np.full((1, 128), -128.0, dtype=BF),
        ones_col_bf=np.ones((128, 1), dtype=BF),
        rms_ones=np.full((128, 1), 1.0 / DI, dtype=BF),
        mneg=mneg,
        sel4=np.ascontiguousarray(
            np.eye(4, dtype=np.float32)[:, :, None] * np.ones((1, 1, 64), np.float32)),
        sel2=np.ascontiguousarray(
            np.eye(2, dtype=np.float32)[:, :, None] * np.ones((1, 1, 128), np.float32)),
        sel2b=np.ascontiguousarray(
            (np.eye(2)[:, :, None] * np.ones((1, 1, 128))).astype(BF)),
        fconst=np.ascontiguousarray(np.broadcast_to(np.array(
            [LNEPS, RMSEPS, -1.0, 1.0, 1.0 / DM, -128.0], np.float32), (128, 6))),
    )

    xf = x.reshape(2 * 4096, DM)
    per_core = []
    for c in range(8):
        b, pos = c // 4, c % 4
        r0 = b * 4096 + pos * T
        xc = np.zeros((TH, DM), np.float32)
        xc[3:] = xf[r0:r0 + T]
        if pos > 0:
            xc[0:3] = xf[r0 - 3:r0]
        m2t = np.zeros((4, 4), np.float32)
        mbias = np.full((4, 1), MASKNEG, np.float32)
        for j in range(4):
            if j < pos:
                mbias[j, 0] = 0.0
                for l in range(j + 1, pos):
                    m2t[l, j] = 1.0
        per_core.append(dict(x=xc, m2t=m2t, mbias=mbias))
    return shared, per_core


class SbufPlan:
    def __init__(self, base, top):
        self.base, self.top = base, top
        self.items = []

    def add(self, name, nbytes, ph0, ph1):
        nbytes = (nbytes + 31) & ~31
        cands = sorted({self.base} | {o + b for (_, o, b, p0, p1) in self.items
                                      if not (p1 < ph0 or p0 > ph1)})
        for off in cands:
            ok = all(p1 < ph0 or p0 > ph1 or off + nbytes <= o or off >= o + b
                     for (_, o, b, p0, p1) in self.items)
            if ok:
                assert off + nbytes <= self.top, \
                    f"sbuf overflow: {name} {nbytes}B at {off} (top {self.top})"
                self.items.append((name, off, nbytes, ph0, ph1))
                return off
        raise AssertionError("unreachable")


class Prog:
    ENGSEM = dict(t="T", v="V", s="S", g="G")

    NQSEM = 6

    def __init__(self):
        self.ops = {e: [] for e in "tvsgq"}
        self.cnt = defaultdict(int)
        self.mark = defaultdict(int)
        self.qidx = {"g": 0, "q": 0}

    def op(self, eng, fn, waits=(), inc=None, incn=1, hard=()):
        own = self.ENGSEM.get(eng)
        rw = []
        for tk in list(waits) + list(hard):
            if tk is None:
                continue
            sem, val = tk
            if sem == own and tk not in hard:
                continue
            if self.mark[(eng, sem)] >= val:
                continue
            self.mark[(eng, sem)] = val
            rw.append((sem, val))
        tok = None
        if inc:
            self.cnt[inc] += incn
            tok = (inc, self.cnt[inc])
        self.ops[eng].append((tuple(rw), fn, inc, incn))
        return tok

    def dma(self, q, out, in_, waits=()):
        base = "GQ" if q == "g" else "SQ"
        sem = f"{base}{self.qidx[q] % self.NQSEM}"
        self.qidx[q] += 1
        self_wait = (sem, self.cnt[sem]) if self.cnt[sem] else None
        return self.op(q, lambda e, o=out, i=in_: e.dma_start(out=o, in_=i),
                       tuple(waits) + (self_wait,), inc=sem, incn=16)

    def mm(self, out, lhsT, rhs, start, stop, waits=(), inc=None):
        return self.op("t", lambda e, o=out, l=lhsT, r=rhs, a=start, b=stop:
                       e.matmul(o, l, r, start=a, stop=b), waits, inc=inc)

    def transp(self, out, in_, ident, waits=(), inc="T"):
        return self.op("t", lambda e, o=out, i=in_, d=ident: e.transpose(o, i, d),
                       waits, inc=inc)

    def act(self, out, in_, func, bias=0.0, scale=1.0, accum=None, waits=(),
            inc="S", hard=()):
        return self.op("s", lambda e, o=out, i=in_, f=func, b=bias, sc=scale, a=accum:
                       e.activation(o, i, f, bias=b, scale=sc, accum_out=a),
                       waits, inc=inc, hard=hard)

    def tt(self, eng, out, a, b, alu, waits=()):
        return self.op(eng, lambda e, o=out, x=a, y=b, u=alu:
                       e.tensor_tensor(o, x, y, u), waits, inc=self.ENGSEM[eng])

    def ts(self, eng, out, in0, s1, s2, op0, op1=ALU.bypass, waits=(), hard=()):
        return self.op(eng, lambda e, o=out, i=in0, a=s1, b=s2, u=op0, w=op1:
                       e.tensor_scalar(o, i, a, b, u, w), waits,
                       inc=self.ENGSEM[eng], hard=hard)

    def stt(self, eng, out, in0, scalar, in1, op0, op1, waits=(), hard=()):
        return self.op(eng, lambda e, o=out, i=in0, sc=scalar, j=in1, u=op0, w=op1:
                       e.scalar_tensor_tensor(o, i, sc, j, u, w),
                       waits, inc=self.ENGSEM[eng], hard=hard)

    def copy(self, eng, out, in_, waits=()):
        return self.op(eng, lambda e, o=out, i=in_: e.tensor_copy(o, i),
                       waits, inc=self.ENGSEM[eng])

    def rsum(self, out, in_, waits=()):
        return self.op("v", lambda e, o=out, i=in_: e.reduce_sum(o, i, AX.X),
                       waits, inc="V")

    def scan(self, out, d0, d1, init, op0, op1, waits=()):
        return self.op("v", lambda e, o=out, a=d0, b=d1, i=init, u=op0, w=op1:
                       e.tensor_tensor_scan(o, a, b, i, u, w), waits, inc="V")

    def recip(self, out, in_, waits=()):
        return self.op("v", lambda e, o=out, i=in_: e.reciprocal(o, i),
                       waits, inc="V")

    def wait_only(self, eng, waits):
        for tk in waits:
            if tk is None:
                continue
            sem, val = tk
            if self.mark[(eng, sem)] >= val:
                continue
            self.mark[(eng, sem)] = val
            self.ops[eng].append((((sem, val),), None, None, 0))


def build_nc(taps=(), trunc=None):
    tapset = {t[0] for t in taps}
    nc = bass.Bass("TRN2", target_bir_lowering=False, debug=False,
                   detect_race_conditions=False)

    din = {}
    for name, shape, dt in [
        ("x", [TH, DM], F32),
        ("win_bc", [33, 128, NKC, 128], F32), ("win_dt", [128, NKC, 32], F32),
        ("win_z", [32, 128, NKC, 128], F32), ("wout", [32, 128, DM], F32),
        ("wg", [128, NKC, 8192], U8), ("wu", [128, NKC, 8192], U8),
        ("wd", [128, 64, DM], U8),
        ("gs_col", [128, 64], F32), ("us_col", [128, 64], F32),
        ("dsc_row", [1, DM], F32),
        ("convw", [128, 33, 4], F32), ("convb", [128, 33], F32),
        ("a_log", [NH, 1], F32), ("dt_bias", [NH, 1], F32), ("d_rep", [128, NH], F32),
        ("ident_bf", [128, 128], BF16), ("ident_f", [128, 128], F32),
        ("ones_row_f", [1, 128], F32), ("ones_row_bf", [1, 128], BF16),
        ("neg128_row", [1, 128], BF16),
        ("ones_col_bf", [128, 1], BF16), ("rms_ones", [128, 1], BF16),
        ("mneg", [128, 2, CH], BF16), ("m2t", [4, 4], F32), ("mbias", [4, 1], F32),
        ("sel4", [4, 4, 64], F32), ("sel2", [2, 2, 128], F32),
        ("sel2b", [2, 2, 128], BF16), ("fconst", [128, 6], F32),
    ]:
        din[name] = nc.dram_tensor(name, shape, dt, kind="ExternalInput").ap()

    y_out = nc.dram_tensor("y_out", [T, DM], F32, kind="ExternalOutput").ap()
    tout = {}
    for tname, shape, dt in taps:
        tout[tname] = nc.dram_tensor("tap_" + tname, shape, dt,
                                     kind="ExternalOutput").ap()

    xn_stash = nc.dram_tensor("xn_stash", [128, NKC, TH], BF16).ap()
    acc_stash = nc.dram_tensor("acc_stash", [128, 8, DM], F32).ap()
    cc_in = nc.dram_tensor("cc_in", [65, 4096], F32).ap()
    cc_out = nc.dram_tensor("cc_out", [260, 4096], F32).ap()

    plan = SbufPlan((nc.sbuf_base + 31) & ~31, nc.sbuf_top)
    S = {}

    def alloc(name, shape, dt, ph0, ph1):
        per_part = int(np.prod(shape[1:])) * mybir.dt.size(dt)
        off = plan.add(name, per_part, ph0, ph1)
        S[name] = nc.alloc_sbuf_tensor_at(name, list(shape), dt, offset=off)

    specs = [
        ("fconst", [128, 6], F32, P0, P7),
        ("identb", [128, 128], BF16, P0, P7), ("identf", [128, 128], F32, P0, P7),
        ("onesrowf", [1, 128], F32, P0, P7), ("onesrowb", [1, 128], BF16, P0, P7),
        ("sel4", [4, 4, 64], F32, P0, P7), ("sel2", [2, 2, 128], F32, P0, P7),
        ("sel2b", [2, 2, 128], BF16, P0, P7),
        ("neg128", [1, 128], BF16, P0, P7),
        ("onescol", [128, 1], BF16, P0, P7), ("rmsones", [128, 1], BF16, P0, P7),
        ("gscol", [128, 64], F32, P0, P1B), ("uscol", [128, 64], F32, P0, P1B),
        ("drep", [128, NH], F32, P0, P7),
        ("alog", [NH, 1], F32, P0, P7), ("acol", [NH, 1], F32, P0, P7),
        ("dtb", [NH, 1], F32, P0, P7),
        ("m2t", [4, 4], F32, P0, P7), ("mbias", [4, 1], F32, P0, P7),
        ("ldcol", [NH, 1], F32, P4A, P4B),
        ("ldrow", [1, NH], F32, P4A, P4A),
        ("mneg", [128, 2, CH], BF16, P0, P4B),
        ("convw", [128, 33, 4], F32, P0, P3), ("convb", [128, 33], F32, P0, P3),
        ("dscrow", [1, DM], F32, P0, P1A),
        ("screp", [128, DM], BF16, P1A, P1B), ("nsd", [1, DM], BF16, P1A, P1B),
        ("rsxn", [1, T], BF16, P1A, P1B), ("rsh", [1, T], BF16, P1B, P1B),
        ("xhalo", [3, DM], F32, P0, P0),
        ("lnscr", [128, DM], F32, P0, P0), ("lnsq", [128, DM], F32, P0, P0),
        ("lncol", [128, 64], F32, P0, P0), ("xntm", [128, DM], BF16, P0, P0),
        ("XN", [128, NKC, TH], BF16, P0, P2),
        ("XN2", [128, NKC, TH], BF16, P5, P5),
        ("ACC", [128, 8, DM], F32, P0, P1B),
        ("ACC2", [128, 8, DM], F32, P6, P7),
        ("XBC", [128, 33, TH], BF16, P2, P6),
        ("wgu", [128, 2, NKC, 512], BF16, P1B, P1B),
        ("tmpg", [128, 4, T], BF16, P1B, P1B),
        ("hb", [128, 8, T], BF16, P1B, P1B),
        ("wd", [128, 8, DM], BF16, P1B, P1B),
        ("ring", [128, 8, NKC, 128], BF16, P2, P2),
        ("ringz", [128, 8, NKC, 128], BF16, P5, P5),
        ("dt", [NH, T], F32, P2, P4A),
        ("dtT", [128, 8, NH], F32, P4A, P4B),
        ("dA", [NH, T], F32, P4A, P4A),
        ("acs", [NH, T], F32, P4A, P4B),
        ("acsT", [128, 8, NH], F32, P4A, P4B),
        ("dcol", [128, 8, NH], F32, P4A, P4A),
        ("decrep", [64, NCH, NH], F32, P4A, P4B),
        ("decrow", [1, NCH, NH], F32, P4A, P4A),
        ("lastrow", [1, NCH, NH], F32, P4A, P4A),
        ("rrow", [2, 2, 2, 128], F32, P4B, P4B),
        ("exprow", [2, 2, 2, 128], BF16, P4B, P4B),
        ("lbf2", [128, 2, 512], BF16, P4B, P4B),
        ("rhs2", [128, 2, 2, 512], BF16, P4B, P4B),
        ("cs0", [128, T], F32, P2, P4A), ("cs1", [128, T], F32, P2, P4A),
        ("xd", [128, 2, 2, NH, HD], BF16, P4A, P4B),
        ("bt", [128, 2, 2, DS], BF16, P4A, P4A),
        ("bd", [128, 4, DS], BF16, P4A, P4A),
        ("states", [64, NCH, DI], BF16, P4A, P4B),
        ("carA", [64, DI], F32, P4A, P4B), ("carB", [64, DI], F32, P4A, P4B),
        ("sload", [64, 1024], F32, P4B, P4B),
        ("cbuf", [64, T], BF16, P3, P4B),
        ("ssb", [128, 2, CH], BF16, P4B, P4B),
        ("ltmp", [128, 2, 512], F32, P4B, P4B),

        ("cx", [64, 2, 2 * CH], BF16, P4B, P4B),
        ("prevbf", [64, 2, 2, HD], BF16, P4B, P4B),
        ("wgt", [4, NH], F32, P4B, P4B), ("wrep", [64, 4, NH], F32, P4B, P4B),
        ("ldg", [4, NH], F32, P4B, P4B),
        ("zrow", [1, 256], F32, P4A, P4A),
        ("ysq", [128, T], BF16, P4B, P4B),
        ("rmsrow", [1, T], F32, P4B, P5),
        ("szt", [128, T], BF16, P5, P5), ("tmpy", [128, T], BF16, P5, P5),
        ("wor0", [128, 8, DM], BF16, P5, P6),
        ("wor1", [128, 8, DM], BF16, P6, P6),
    ]
    specs.sort(key=lambda a: -int(np.prod(a[1][1:])) * mybir.dt.size(a[2]))
    for args in specs:
        alloc(*args)

    pb = [nc.alloc_psum_tensor(f"pb{i}", [128, 512], F32) for i in range(8)]

    def pbf(i):
        return pb[i].ap().bitcast(BF16)

    p = Prog()
    marks = {}

    def mark(name):
        marks[name] = {e: len(p.ops[e]) for e in p.ops}

    src_of = {"identb": "ident_bf", "identf": "ident_f", "onesrowf": "ones_row_f",
              "onesrowb": "ones_row_bf", "sel4": "sel4",
              "neg128": "neg128_row", "onescol": "ones_col_bf", "rmsones": "rms_ones",
              "gscol": "gs_col", "uscol": "us_col", "drep": "d_rep", "alog": "a_log",
              "dtb": "dt_bias", "m2t": "m2t", "mbias": "mbias", "mneg": "mneg",
              "sel2": "sel2", "sel2b": "sel2b",
              "convw": "convw", "convb": "convb", "dscrow": "dsc_row",
              "fconst": "fconst"}
    for n, srcn in src_of.items():
        consts_tok = p.dma("q", S[n][:], din[srcn][:])
    for i, v in enumerate([LNEPS, RMSEPS, -1.0, 1.0, 1.0 / DM, -128.0]):
        nc.const_aps.register(F32, v, S["fconst"][:, i:i + 1])

    xmain = din["x"][3:TH, :].rearrange("(tl q) d -> q tl d", q=128)
    tok_x = p.dma("q", S["ACC"][:], xmain)
    tok_xh = p.dma("q", S["xhalo"][:], din["x"][0:3, :])

    xn_evacs = []
    last_tp = None
    for t in range(9):
        if t < 8:
            xin, npt, w = S["ACC"][:, t, :], 128, (tok_x,)
        else:
            xin, npt, w = S["xhalo"][:], 3, (tok_xh,)
        col = S["lncol"]
        c0 = t * 6
        tv = p.rsum(col[0:npt, c0:c0 + 1], xin, waits=w)
        tv = p.act(col[0:npt, c0 + 1:c0 + 2], col[0:npt, c0:c0 + 1], AF.Copy,
                   scale=-1.0 / DM, waits=(tv,))
        ts1 = p.ts("v", S["lnscr"][0:npt, :], xin, col[0:npt, c0 + 1:c0 + 2],
                   None, ALU.add, waits=(tv,))
        ts2 = p.act(S["lnsq"][0:npt, :], S["lnscr"][0:npt, :], AF.Square,
                    accum=col[0:npt, c0 + 2:c0 + 3], waits=(ts1,))
        ts3 = p.act(col[0:npt, c0 + 3:c0 + 4], col[0:npt, c0 + 2:c0 + 3], AF.Sqrt,
                    scale=1.0 / DM, bias=LNEPS, waits=(ts2,))
        tv2 = p.recip(col[0:npt, c0 + 4:c0 + 5], col[0:npt, c0 + 3:c0 + 4],
                      waits=(ts3,))
        ts4 = p.act(S["xntm"][0:npt, :], S["lnscr"][0:npt, :], AF.Copy,
                    scale=col[0:npt, c0 + 4:c0 + 5],
                    waits=(tv2, consts_tok, last_tp))
        for kc in range(NKC):
            bank = 6 + (kc % 2)
            guard = xn_evacs[-2] if len(xn_evacs) >= 2 else None
            last_tp = p.transp(pbf(bank)[0:128, 0:npt],
                               S["xntm"][0:npt, bass.ts(kc, 128)],
                               S["identb"][0:npt, 0:npt], waits=(ts4, guard))
            dst = (S["XN"][:, kc, 3 + t * 128: 3 + (t + 1) * 128] if t < 8
                   else S["XN"][:, kc, 0:3])
            xn_evacs.append(p.act(dst, pbf(bank)[0:128, 0:npt], AF.Copy,
                                  waits=(last_tp,)))
    xn_tok = xn_evacs[-1]
    if "xn" in tapset:
        p.dma("q", tout["xn"][:], S["XN"][:], waits=(xn_tok,))

    mark("P0")
    sr_tok = None
    for n in range(4):
        tm = p.mm(pb[4][0:128, 0:512], S["onesrowf"][0:1, :],
                  S["dscrow"][0:1, bass.ts(n, 512)], True, True,
                  waits=(consts_tok, sr_tok, xn_tok), inc="T")
        sr_tok = p.copy("v", S["screp"][:, bass.ts(n, 512)], pb[4][0:128, 0:512],
                        waits=(tm,))
    tnsd = p.act(S["nsd"][:], S["dscrow"][:], AF.Copy, scale=-128.0,
                 waits=(consts_tok,))
    rsxn_tok = None
    for n in range(2):
        tm = None
        for kc in range(NKC):
            tm = p.mm(pb[6][0:1, 0:512], S["onescol"][:],
                      S["XN"][:, kc, 3 + n * 512: 3 + (n + 1) * 512],
                      kc == 0, kc == NKC - 1,
                      waits=(xn_tok, consts_tok, rsxn_tok) if kc == 0 else (),
                      inc="T" if kc == NKC - 1 else None)
        rsxn_tok = p.act(S["rsxn"][0:1, bass.ts(n, 512)], pb[6][0:1, 0:512],
                         AF.Copy, waits=(tm,))

    mark("P1A")
    ffn_last = None
    hrow_last = None
    wd_free = None
    hb_free = None
    wgu_last = [None, None]
    dn_prev = None
    wg_pref = None
    bank_guard = {0: None, 2: None}
    for pair in range(8):
        twd = p.dma("g", S["wd"][:], din["wd"][:, pair * 8:(pair + 1) * 8, :],
                    waits=(wd_free, sr_tok, xn_tok))
        wdsc = None
        for ic in range(8):
            wdsc = p.tt("v", S["wd"][:, ic, :], S["wd"][:, ic, :], S["screp"][:],
                        ALU.mult, waits=(twd,))
        gate_ev = {}
        h_tok = None
        up_prev_sb = None
        for sbi in range(2):
            for mat in range(2):
                slot = (sbi * 2 + mat) % 2
                wsrc = din["wg"] if mat == 0 else din["wu"]
                cols = slice((pair * 2 + sbi) * 512, (pair * 2 + sbi + 1) * 512)
                if pair > 0 and sbi == 0 and mat == 0:
                    wt = wg_pref
                else:
                    wt = p.dma("g", S["wgu"][:, slot, :, :], wsrc[:, :, cols],
                               waits=(wgu_last[slot], xn_tok))
                for m in range(4):
                    ic = sbi * 4 + m
                    gic = pair * 8 + ic
                    b0, b1 = (0, 1) if m % 2 == 0 else (2, 3)
                    first_w = [wt, xn_tok, dn_prev, up_prev_sb, bank_guard[b0]]
                    if mat == 1:
                        first_w += [gate_ev[(m, 0)], gate_ev[(m, 1)]]
                    for kc in range(NKC):
                        st = kc == 0
                        p.mm(pb[b0][0:128, 0:512],
                             S["wgu"][:, slot, kc, bass.ts(m, 128)],
                             S["XN"][:, kc, 3:515], st, False,
                             waits=tuple(first_w) if kc == 0 else ())
                        p.mm(pb[b1][0:128, 0:512],
                             S["wgu"][:, slot, kc, bass.ts(m, 128)],
                             S["XN"][:, kc, 515:1027], st, False)
                    p.mm(pb[b0][0:128, 0:512], S["neg128"][:],
                         S["rsxn"][0:1, 0:512], False, True, waits=(rsxn_tok,))
                    tm = p.mm(pb[b1][0:128, 0:512], S["neg128"][:],
                              S["rsxn"][0:1, 512:1024], False, True, inc="T")
                    wgu_last[slot] = tm
                    if mat == 0:
                        gate_ev[(m, 0)] = p.act(
                            S["tmpg"][:, m, 0:512], pb[b0][0:128, 0:512], AF.Silu,
                            scale=S["gscol"][:, gic:gic + 1],
                            waits=(tm, up_prev_sb))
                        gate_ev[(m, 1)] = p.act(
                            S["tmpg"][:, m, 512:1024], pb[b1][0:128, 0:512],
                            AF.Silu, scale=S["gscol"][:, gic:gic + 1], waits=(tm,))
                        bank_guard[b0] = gate_ev[(m, 1)]
                    else:
                        p.stt("v", S["hb"][:, ic, 0:512], pb[b0][0:128, 0:512],
                              S["uscol"][:, gic:gic + 1], S["tmpg"][:, m, 0:512],
                              ALU.mult, ALU.mult, waits=(tm, hb_free))
                        h_tok = p.stt("v", S["hb"][:, ic, 512:1024],
                                      pb[b1][0:128, 0:512],
                                      S["uscol"][:, gic:gic + 1],
                                      S["tmpg"][:, m, 512:1024],
                                      ALU.mult, ALU.mult, waits=(tm,))
                        bank_guard[b0] = h_tok
            up_prev_sb = h_tok
        for n in range(2):
            for ic in range(8):
                st = pair == 0 and ic == 0
                sp = pair == 7 and ic == 7
                tk = p.mm(pb[4 + n][0:1, 0:512], S["onescol"][:],
                          S["hb"][:, ic, bass.ts(n, 512)], st, sp,
                          waits=(h_tok,) if ic == 0 else (),
                          inc="T" if sp else None)
                if sp:
                    hrow_last = tk
        for t in range(8):
            b0, b1 = (0, 1) if t % 2 == 0 else (2, 3)
            for nb in range(2):
                tm = None
                for ic in range(8):
                    st = ic == 0
                    sp = ic == 7
                    p.mm(pb[b0][0:128, 0:512], S["hb"][:, ic, bass.ts(t, 128)],
                         S["wd"][:, ic, nb * 1024: nb * 1024 + 512], st, sp,
                         waits=(wdsc, h_tok, bank_guard[b0]) if ic == 0 else ())
                    tm = p.mm(pb[b1][0:128, 0:512], S["hb"][:, ic, bass.ts(t, 128)],
                              S["wd"][:, ic, nb * 1024 + 512:(nb + 1) * 1024],
                              st, sp, inc="T" if sp else None)
                p.tt("v", S["ACC"][:, t, nb * 1024: nb * 1024 + 512],
                     S["ACC"][:, t, nb * 1024: nb * 1024 + 512],
                     pb[b0][0:128, 0:512], ALU.add, waits=(tm, tok_x))
                ffn_last = p.tt("v", S["ACC"][:, t, nb * 1024 + 512:(nb + 1) * 1024],
                                S["ACC"][:, t, nb * 1024 + 512:(nb + 1) * 1024],
                                pb[b1][0:128, 0:512], ALU.add, waits=(tm,))
                bank_guard[b0] = ffn_last
        dn_prev = ffn_last
        wd_free = ffn_last
        hb_free = ffn_last
        if pair < 7:
            ncols = slice((pair + 1) * 2 * 512, ((pair + 1) * 2 + 1) * 512)
            wg_pref = p.dma("g", S["wgu"][:, 0, :, :], din["wg"][:, :, ncols],
                            waits=(wgu_last[0], xn_tok))

    e0 = p.act(S["rsh"][0:1, 0:512], pb[4][0:1, 0:512], AF.Copy, waits=(hrow_last,))
    e1 = p.act(S["rsh"][0:1, 512:1024], pb[5][0:1, 0:512], AF.Copy,
               waits=(hrow_last,))
    corr_prev = ffn_last
    acc_stash_tok = None
    acc_toks = []
    for t in range(8):
        for n in range(4):
            bank = n % 2
            tm = p.mm(pb[bank][0:128, 0:512], S["rsh"][0:1, bass.ts(t, 128)],
                      S["nsd"][0:1, bass.ts(n, 512)], True, True,
                      waits=(e0, e1, tnsd, corr_prev), inc="T")
            corr_prev = p.tt("v", S["ACC"][:, t, bass.ts(n, 512)],
                             S["ACC"][:, t, bass.ts(n, 512)],
                             pb[bank][0:128, 0:512], ALU.add, waits=(tm,))
        acc_stash_tok = p.dma("q", acc_stash[:, t, :], S["ACC"][:, t, :],
                              waits=(corr_prev,) +
                              (tuple(acc_toks) if t == 7 else ()))
        acc_toks.append(acc_stash_tok)
    ffn_tok = corr_prev
    if "ffn" in tapset:
        p.dma("q", tout["ffn"][:], S["ACC"][:], waits=(ffn_tok,))

    mark("P1B")
    ring_free = [None] * 8
    xbc_evac = []
    ev_pair = {0: None, 2: None}
    evh_prev = None
    dt_tok = None
    conv_done = []
    for mi in range(34):
        m = 33 if mi == 0 else mi - 1
        slot = mi % 8
        wsrc = din["win_bc"][m] if m < 33 else din["win_dt"]
        wdst = S["ring"][:, slot, :, :] if m < 33 else S["ring"][:, slot, :, 0:32]
        tw = p.dma("g", wdst, wsrc, waits=(ring_free[slot], acc_stash_tok))
        b0, b1 = (0, 1) if mi % 2 == 0 else (2, 3)
        mw = 128 if m < 33 else 32
        tm = None
        for kc in range(NKC):
            st, sp = kc == 0, kc == NKC - 1
            lhs = S["ring"][:, slot, kc, 0:mw]
            p.mm(pb[b0][0:mw, 0:512], lhs, S["XN"][:, kc, 3:515], st, sp,
                 waits=(tw, xn_tok, ev_pair[b0]) if kc == 0 else ())
            tm = p.mm(pb[b1][0:mw, 0:512], lhs, S["XN"][:, kc, 515:1027], st, sp,
                      inc="T" if sp else None)
        if m < 33:
            th = None
            for kc in range(NKC):
                th = p.mm(pb[6][0:mw, 0:3], S["ring"][:, slot, kc, 0:mw],
                          S["XN"][:, kc, 0:3], kc == 0, kc == NKC - 1,
                          waits=(evh_prev,) if kc == 0 else (),
                          inc="T" if kc == NKC - 1 else None)
            ring_free[slot] = th
            ev0 = p.act(S["XBC"][:, m, 3:515], pb[b0][0:128, 0:512], AF.Copy,
                        waits=(tm,))
            ev1 = p.act(S["XBC"][:, m, 515:1027], pb[b1][0:128, 0:512], AF.Copy,
                        waits=(tm,))
            evh_prev = p.act(S["XBC"][:, m, 0:3], pb[6][0:128, 0:3], AF.Copy,
                             waits=(th,))
            ev_pair[b0] = ev1
            xbc_evac.append((ev0, ev1, evh_prev))
            w = (ev0, ev1, evh_prev) + ((conv_done[-1],) if conv_done else ())
            t0c = p.ts("v", S["cs0"][:], S["XBC"][:, m, 0:T], S["convw"][:, m, 0:1],
                       S["convb"][:, m:m + 1], ALU.mult, ALU.add, waits=w)
            t1c = p.stt("v", S["cs1"][:], S["XBC"][:, m, 1:T + 1],
                        S["convw"][:, m, 1:2], S["cs0"][:], ALU.mult, ALU.add,
                        waits=(t0c,))
            t2c = p.stt("v", S["cs0"][:], S["XBC"][:, m, 2:T + 2],
                        S["convw"][:, m, 2:3], S["cs1"][:], ALU.mult, ALU.add,
                        waits=(t1c,))
            t3c = p.stt("v", S["cs1"][:], S["XBC"][:, m, 3:T + 3],
                        S["convw"][:, m, 3:4], S["cs0"][:], ALU.mult, ALU.add,
                        waits=(t2c,))
            conv_done.append(p.act(S["XBC"][:, m, 0:T], S["cs1"][:], AF.Silu,
                                   waits=(t3c,)))
        else:
            ring_free[slot] = tm
            last_a = None
            for half, bank in ((0, b0), (1, b1)):
                cols = slice(half * 512, (half + 1) * 512)
                r = p.act(S["cs0"][0:NH, cols], pb[bank][0:NH, 0:512], AF.Relu,
                          bias=S["dtb"][:], waits=(tm,))
                a = p.act(S["cs1"][0:NH, cols], pb[bank][0:NH, 0:512], AF.Abs,
                          bias=S["dtb"][:], waits=(tm,))
                last_a = a
                e = p.act(S["cs1"][0:NH, cols], S["cs1"][0:NH, cols], AF.Exp,
                          scale=-1.0, waits=(a,))
                l = p.act(S["cs1"][0:NH, cols], S["cs1"][0:NH, cols], AF.Ln,
                          bias=1.0, waits=(e,))
                dt_tok = p.tt("v", S["dt"][0:NH, cols], S["cs0"][0:NH, cols],
                              S["cs1"][0:NH, cols], ALU.add, waits=(l, r))
            ev_pair[b0] = last_a
    xn_stash_tok = p.dma("q", xn_stash[:], S["XN"][:], waits=(dt_tok, tm))
    tap_hold = None
    if "xbc" in tapset:
        tap_hold = p.dma("q", tout["xbc"][:], S["XBC"][:],
                         waits=(xbc_evac[-1][0], xbc_evac[-1][1]))

    conv_tok = conv_done[-1]
    if "xc" in tapset:
        p.dma("q", tout["xc"][:], S["XBC"][:, :, 0:T], waits=(conv_tok,))
    cb_tok = p.copy("v", S["cbuf"][:], S["XBC"][64:128, 32, 0:T], waits=(conv_tok,))

    mark("P3")
    tae = p.act(S["acol"][:], S["alog"][:], AF.Exp, waits=(consts_tok,))
    tac = p.act(S["acol"][:], S["acol"][:], AF.Copy, scale=-1.0, waits=(tae,))
    tda = p.ts("v", S["dA"][:], S["dt"][:], S["acol"][:], None, ALU.mult,
               waits=(tac, dt_tok, xn_stash_tok, acc_stash_tok))
    acs_tok = None
    for c in range(NCH):
        cols = slice(c * CH, (c + 1) * CH)
        acs_tok = p.scan(S["acs"][:, cols], S["dA"][:, cols], S["dA"][:, cols],
                         0.0, ALU.add, ALU.bypass, waits=(tda,))
    ld_tok = None
    tp_prev2 = [None, None]
    tp_toks = []
    for grp, srcbuf, dstbuf, wtok in ((0, "dt", "dtT", dt_tok),
                                      (1, "acs", "acsT", acs_tok)):
        for i in range(8):
            bank = 6 + i % 2
            tt = p.transp(pb[bank][0:128, 0:NH], S[srcbuf][:, bass.ts(i, 128)],
                          S["identf"][0:NH, 0:NH],
                          waits=(wtok, tp_prev2[i % 2], xn_stash_tok,
                                 acc_stash_tok))
            if dstbuf == "dtT":
                tp_prev2[i % 2] = p.copy("v", S[dstbuf][:, i, :],
                                         pb[bank][0:128, 0:NH], waits=(tt,))
            else:
                tp_prev2[i % 2] = p.act(S[dstbuf][:, i, :], pb[bank][0:128, 0:NH],
                                        AF.Copy, waits=(tt,))
            tp_toks.append(tp_prev2[i % 2])
    acst_tok = tp_toks[-1]
    dtT_tok = tp_toks[7]

    dcol_tok = None
    dec_tok = None
    lr_prev = None
    dr_prev = None
    last_ev_prev = acst_tok
    for c in range(NCH):
        tlt = p.transp(pb[3][0:1, 0:NH],
                       S["acs"][:, (c + 1) * CH - 1:(c + 1) * CH],
                       S["identf"][0:NH, 0:NH],
                       waits=(acs_tok, last_ev_prev, dr_prev))
        tle = p.act(S["lastrow"][0:1, c, :], pb[3][0:1, 0:NH], AF.Copy,
                    waits=(tlt,))
        last_ev_prev = tle
        tm = p.mm(pb[2][0:128, 0:NH], S["onesrowf"][0:1, :],
                  S["lastrow"][0:1, c, :], True, True,
                  waits=(tle, lr_prev), inc="T")
        for j in range(2):
            buf = "cs0" if j == 0 else "cs1"
            tv = p.tt("v", S[buf][0:128, 0:NH], pb[2][0:128, 0:NH],
                      S["acsT"][:, 2 * c + j, :], ALU.subtract,
                      waits=(tm, acst_tok))
            dcol_tok = p.act(S["dcol"][:, 2 * c + j, :], S[buf][0:128, 0:NH],
                             AF.Exp, waits=(tv,))
            lr_prev = tv
        trr = p.act(S["decrow"][0:1, c, :], S["lastrow"][0:1, c, :],
                    AF.Exp, waits=(tle,))
        tm2 = p.mm(pb[3][0:64, 0:NH], S["onesrowf"][0:1, 0:64],
                   S["decrow"][0:1, c, :], True, True,
                   waits=(trr, dr_prev, lr_prev), inc="T")
        dec_tok = p.act(S["decrep"][:, c, :], pb[3][0:64, 0:NH], AF.Copy,
                        waits=(tm2,))
        dr_prev = dec_tok

    ld_tok = p.tt("v", S["ldrow"][:], S["lastrow"][0:1, 0, :],
                  S["lastrow"][0:1, 1, :], ALU.add, waits=(last_ev_prev,))
    ld_tok = p.tt("v", S["ldrow"][:], S["ldrow"][:], S["lastrow"][0:1, 2, :],
                  ALU.add, waits=(ld_tok,))
    ld_tok = p.tt("v", S["ldrow"][:], S["ldrow"][:], S["lastrow"][0:1, 3, :],
                  ALU.add, waits=(ld_tok,))

    stash_fence = (xn_stash_tok, acc_stash_tok)
    xd_state = {"last_evac": acst_tok}

    def emit_xd(c, buf, extra=()):
        toks = []
        for j in range(2):
            for h in range(NH):
                bank = 6 + (h % 2)
                guard = toks[-2] if len(toks) >= 2 else xd_state["last_evac"]
                tt = p.transp(pbf(bank)[0:128, 0:128],
                              S["XBC"][:, h, c * CH + j * 128: c * CH + (j + 1) * 128],
                              S["identb"][:],
                              waits=(conv_tok, guard) + stash_fence + tuple(extra))
                toks.append(p.act(S["xd"][:, buf, j, h, :], pbf(bank)[0:128, 0:128],
                                  AF.Copy, scale=S["dtT"][:, 2 * c + j, h:h + 1],
                                  waits=(tt, dtT_tok)))
        xd_state["last_evac"] = toks[-1]
        return toks[-1]

    def emit_bt(c, buf):
        toks = []
        for j in range(2):
            bank = 6 + j
            tt = p.transp(pbf(bank)[0:128, 0:64],
                          S["XBC"][0:64, 32, c * CH + j * 128: c * CH + (j + 1) * 128],
                          S["identb"][0:64, 0:64],
                          waits=(conv_tok, xd_state["last_evac"],
                                 toks[-1] if toks else None))
            toks.append(p.act(S["bt"][:, buf, j, :], pbf(bank)[0:128, 0:64],
                              AF.Copy, waits=(tt,)))
        xd_state["last_evac"] = toks[-1]
        return toks[-1]

    mark("P4A_T")
    scan1_tok = None
    st_ev = None
    prev_tmm = None
    st_bank_rd = [dec_tok, dec_tok]
    bd_guard = [None] * 4
    for c in range(NCH):
        mark(f"P4A_C{c}")
        buf = c % 2
        xd_tok = emit_xd(c, buf)
        if c == 0:
            mark("P4A_XD")
            if "xd0" in tapset:
                tk = p.dma("q", tout["xd0"][:], S["xd"][:, 0, :, :, :],
                           waits=(xd_tok,))
                xd_state["last_evac"] = tk
        bt_tok = emit_bt(c, buf)
        if c == 0:
            mark("P4A_BT")
            if "bt0" in tapset:
                tk = p.dma("q", tout["bt0"][:], S["bt"][:, 0, :, :],
                           waits=(bt_tok,))
                xd_state["last_evac"] = tk
        for h in range(NH):
            if c == 0 and h in (1, 2, 16):
                mark(f"P4A_H{h}")
            bank = h % 2
            tmm = None
            for j in range(2):
                sl = (h % 2) * 2 + j
                tbd = p.ts("v", S["bd"][:, sl, :], S["bt"][:, buf, j, :],
                           S["dcol"][:, 2 * c + j, h:h + 1], None, ALU.mult,
                           waits=(bt_tok, dcol_tok, bd_guard[sl]))
                tmm = p.mm(pb[bank][0:64, 0:128], S["bd"][:, sl, :],
                           S["xd"][:, buf, j, h, :], j == 0, j == 1,
                           waits=(tbd, xd_tok, st_bank_rd[bank]), inc="T")
                bd_guard[sl] = tmm
            prev_tmm = tmm
            st_ev = p.copy("v", S["states"][:, c, h * HD:(h + 1) * HD],
                           pb[bank][0:64, 0:128], waits=(tmm,))
            st_bank_rd[bank] = st_ev
            if c == 0:
                scan1_tok = p.copy("v", S["carA"][:, h * HD:(h + 1) * HD],
                                   pb[bank][0:64, 0:128], waits=(tmm,))
                st_bank_rd[bank] = scan1_tok
        if c > 0:
            prevb = "carA" if c % 2 == 1 else "carB"
            nxtb = "carB" if c % 2 == 1 else "carA"
            for h in range(NH):
                scan1_tok = p.stt("v", S[nxtb][:, h * HD:(h + 1) * HD],
                                  S[prevb][:, h * HD:(h + 1) * HD],
                                  S["decrep"][:, c, h:h + 1],
                                  S["states"][:, c, h * HD:(h + 1) * HD],
                                  ALU.mult, ALU.add, waits=(st_ev, dec_tok))
    if "dt" in tapset:
        p.dma("q", tout["dt"][:], S["dt"][:], waits=(dt_tok,))
    if "ldc" in tapset:
        p.dma("q", tout["ldc"][:], S["ldrow"][:], waits=(ld_tok,))
    if "dcol" in tapset:
        p.dma("q", tout["dcol"][:], S["dcol"][:], waits=(dcol_tok,))
    if "lrow" in tapset:
        p.dma("q", tout["lrow"][:], S["lastrow"][:], waits=(dcol_tok,))
    if "acs" in tapset:
        p.dma("q", tout["acs"][:], S["acs"][:], waits=(acs_tok,))
    if "states" in tapset:
        p.dma("q", tout["states"][:], S["states"][:], waits=(scan1_tok,))
    if "score" in tapset:
        p.dma("q", tout["score"][:], S["carB"][:], waits=(scan1_tok,))
    mark("P4A_PRE")
    tz = p.op("v", lambda e, a=S["zrow"][:]: e.memset(a, 0.0),
              (xn_stash_tok, acc_stash_tok), inc="V")
    tz2 = p.copy("v", S["zrow"][0:1, 0:NH], S["ldrow"][:], waits=(tz, ld_tok))
    t1 = p.dma("q", cc_in[0:64, :], S["carB"][:], waits=(scan1_tok,))
    t2 = None
    for seg in range(16):
        t2 = p.dma("q", cc_in[64:65, 256 * seg:256 * (seg + 1)],
                   S["zrow"][:], waits=(tz2,))
    cc_tok = p.op("g", lambda g: g.collective_compute(
        "AllGather", ALU.bypass, replica_groups=[[0, 1, 2, 3], [4, 5, 6, 7]],
        ins=[cc_in.opt()], outs=[cc_out.opt()]), waits=(t1, t2), inc="G")

    mark("P4A")
    ldg_toks = [p.dma("q", S["ldg"][j:j + 1, :],
                      cc_out[65 * j + 64:65 * j + 65, 0:NH], waits=(cc_tok,))
                for j in range(4)]
    tm = p.mm(pb[2][0:4, 0:NH], S["m2t"][:], S["ldg"][:], True, True,
              waits=tuple(ldg_toks) + (consts_tok, lr_prev), inc="T")
    tw = p.act(S["wgt"][:], pb[2][0:4, 0:NH], AF.Exp, bias=S["mbias"][:],
               waits=(tm,))
    wr_tok = None
    for j in range(4):
        tm = p.mm(pb[3][0:64, 0:NH], S["sel4"][:, j, :], S["wgt"][:],
                  True, True, waits=(tw, dec_tok if j == 0 else wr_tok), inc="T")
        wr_tok = p.act(S["wrep"][:, j, :], pb[3][0:64, 0:NH], AF.Copy,
                       waits=(tm,))
    if "wgt0" in tapset:
        wr_tok = p.dma("q", tout["wgt0"][:], S["wgt"][:], waits=(tw, wr_tok))
        wr_tok = p.dma("q", tout["ldg0"][:], S["ldg"][:], waits=(wr_tok,))
        wr_tok = p.dma("q", tout["wrp0"][:], S["wrep"][:], waits=(wr_tok,))
    comb_tok = None
    prev_load = None
    comb_half = [None] * 4
    for half in range(4):
        for j in range(4):
            tload = p.dma("q", S["sload"][:],
                          cc_out[65 * j:65 * j + 64, half * 1024:(half + 1) * 1024],
                          waits=(cc_tok, prev_load))
            for hh in range(NH // 4):
                h = half * (NH // 4) + hh
                if j == 0:
                    comb_tok = p.ts("v", S["carA"][:, h * HD:(h + 1) * HD],
                                    S["sload"][:, hh * HD:(hh + 1) * HD],
                                    S["wrep"][:, 0, h:h + 1], None, ALU.mult,
                                    waits=(tload, scan1_tok, wr_tok))
                else:
                    src = "carA" if j % 2 == 1 else "carB"
                    dst = "carB" if j % 2 == 1 else "carA"
                    comb_tok = p.stt("v", S[dst][:, h * HD:(h + 1) * HD],
                                     S["sload"][:, hh * HD:(hh + 1) * HD],
                                     S["wrep"][:, j, h:h + 1],
                                     S[src][:, h * HD:(h + 1) * HD],
                                     ALU.mult, ALU.add, waits=(tload, wr_tok))
            prev_load = comb_tok
        comb_half[half] = comb_tok
    car = "carB"
    if "cin" in tapset:
        tap_cin = p.dma("q", tout["cin"][:], S["carB"][:], waits=(comb_tok,))
        comb_tok = tap_cin

    y_tok = None
    s_prev = None
    prev_ymm = [None, None]
    prev_tl = [lr_prev, lr_prev]
    prev_te = [None, None]
    prev_rhs = [None, None]
    prev_tcx = [wr_tok, wr_tok]
    pv_guard = [[None, None], [None, None]]
    prev_tce = [None, None]
    prev_tr = None
    rrow_guard = None
    y_bank_prev = [scan1_tok, scan1_tok]
    for c in range(NCH):
        buf = c % 2
        xd_tok = emit_xd(c, buf)
        rrow_guard = rrow_guard or xd_tok
        s_toks = []
        for j in range(2):
            tmS = p.mm(pb[j][0:128, 0:CH],
                       S["XBC"][0:64, 32, c * CH + j * 128: c * CH + (j + 1) * 128],
                       S["cbuf"][:, c * CH:(c + 1) * CH], True, True,
                       waits=(conv_tok, cb_tok, s_prev,
                              y_bank_prev[j]), inc="T")
            s_toks.append(p.act(S["ssb"][:, j, :], pb[j][0:128, 0:CH], AF.Copy,
                                waits=(tmS,)))
            s_prev = s_toks[-1]
            y_bank_prev[j] = s_toks[-1]
        for hp in range(NH // 2):
            par = hp % 2
            h0 = hp * 2
            rev = None
            for j in range(2):
                bank = 6 + j
                ttr = p.transp(pb[bank][0:2, 0:128],
                               S["acsT"][:, 2 * c + j, h0:h0 + 2], S["identf"][:],
                               waits=(acst_tok, rrow_guard, xd_tok))
                rev = p.act(S["rrow"][0:2, par, j, :], pb[bank][0:2, 0:128],
                            AF.Copy, waits=(ttr, prev_tr))
            rrow_guard = rev
            tex = p.act(S["exprow"][0:2, par, :, :], S["rrow"][0:2, par, :, :],
                        AF.Exp, waits=(rev, prev_tce[par]))
            p.mm(pb[2 + par][0:128, 0:CH], S["sel2"][:, 0, :],
                 S["rrow"][0:2, par, :, :].rearrange("p a b -> p (a b)"),
                 True, False, waits=(rev, prev_tl[par]))
            tr = p.mm(pb[2 + par][0:128, CH:2 * CH], S["sel2"][:, 1, :],
                      S["rrow"][0:2, par, :, :].rearrange("p a b -> p (a b)"),
                      False, True, inc="T")
            prev_tr = tr
            p.mm(pb[4 + par][0:64, 0:CH], S["sel2b"][:, 0, 0:64],
                 S["exprow"][0:2, par, :, :].rearrange("p a b -> p (a b)"),
                 True, False, waits=(tex, prev_tcx[par]))
            tce = p.mm(pb[4 + par][0:64, CH:2 * CH], S["sel2b"][:, 1, 0:64],
                       S["exprow"][0:2, par, :, :].rearrange("p a b -> p (a b)"),
                       False, True, inc="T")
            prev_tce[par] = tce
            tcx = p.tt("v", S["cx"][:, par, :].rearrange("p (a b) -> p a b", a=2),
                       S["cbuf"][:, c * CH:(c + 1) * CH].rearrange(
                           "p (a b) -> p a b", a=1).to_broadcast([64, 2, CH]),
                       pb[4 + par][0:64, 0:2 * CH].rearrange(
                           "p (a b) -> p a b", a=2),
                       ALU.mult, waits=(tce, cb_tok, prev_ymm[par]))
            prev_tcx[par] = tcx
            for j in range(2):
                tla = p.stt("v", S["ltmp"][:, par, 0:CH], pb[2 + par][0:128, 0:CH],
                            S["acsT"][:, 2 * c + j, h0:h0 + 1], S["mneg"][:, j, :],
                            ALU.subtract, ALU.add,
                            waits=(tr, acst_tok, consts_tok, prev_te[par]))
                tlb = p.stt("v", S["ltmp"][:, par, CH:2 * CH],
                            pb[2 + par][0:128, CH:2 * CH],
                            S["acsT"][:, 2 * c + j, h0 + 1:h0 + 2],
                            S["mneg"][:, j, :], ALU.subtract, ALU.add,
                            waits=(tla,))
                prev_tl[par] = tlb
                te = p.act(S["lbf2"][:, par, :], S["ltmp"][:, par, :], AF.Exp,
                           waits=(tlb, prev_rhs[par]))
                prev_te[par] = te
                trh = p.tt("v", S["rhs2"][:, par, j, :].rearrange(
                               "p (a b) -> p a b", a=2),
                           S["ssb"][:, j, :].rearrange(
                               "p (a b) -> p a b", a=1).to_broadcast([128, 2, CH]),
                           S["lbf2"][:, par, :].rearrange("p (a b) -> p a b", a=2),
                           ALU.mult, waits=(te, s_toks[j], prev_ymm[par]))
                prev_rhs[par] = trh
            for dh in range(2):
                h = h0 + dh
                tpv = p.copy("v", S["prevbf"][:, par, dh, :],
                             S[car][:, h * HD:(h + 1) * HD],
                             waits=(comb_half[h // 8], pv_guard[par][dh]))
                p.mm(pb[par][0:128, 0:CH], S["xd"][:, buf, 0, h, :],
                     S["rhs2"][:, par, 0, dh * CH:(dh + 1) * CH], True, False,
                     waits=(prev_rhs[par], xd_tok, y_bank_prev[par]))
                p.mm(pb[par][0:128, 0:CH], S["xd"][:, buf, 1, h, :],
                     S["rhs2"][:, par, 1, dh * CH:(dh + 1) * CH], False, False)
                tmy = p.mm(pb[par][0:128, 0:CH], S["prevbf"][:, par, dh, :],
                           S["cx"][:, par, dh * CH:(dh + 1) * CH],
                           False, True, waits=(tpv, tcx), inc="T")
                prev_ymm[par] = tmy
                pv_guard[par][dh] = tmy
                ysl = S["XBC"][:, h, c * CH:(c + 1) * CH]
                y_tok = p.stt("v", ysl, ysl, S["drep"][:, h:h + 1],
                              pb[par][0:128, 0:CH], ALU.mult, ALU.add,
                              waits=(tmy, consts_tok))
                y_bank_prev[par] = y_tok
        xd_state["last_evac"] = rrow_guard
        if c < NCH - 1:
            nxt = "carA" if car == "carB" else "carB"
            for h in range(NH):
                comb_tok = p.stt("v", S[nxt][:, h * HD:(h + 1) * HD],
                                 S[car][:, h * HD:(h + 1) * HD],
                                 S["decrep"][:, c, h:h + 1],
                                 S["states"][:, c, h * HD:(h + 1) * HD],
                                 ALU.mult, ALU.add, waits=(y_tok, dec_tok))
            car = nxt
    tap_hold2 = None
    if "yssd" in tapset:
        tap_hold2 = p.dma("q", tout["yssd"][:], S["XBC"][:, 0:32, 0:T],
                          waits=(y_tok,))

    rms_last = None
    tq = None
    for hc in range(NH):
        tq = p.tt("v", S["ysq"][:], S["XBC"][:, hc, 0:T], S["XBC"][:, hc, 0:T],
                  ALU.mult, waits=(y_tok, rms_last) if hc == 0 else (y_tok,))
        for n in range(2):
            st, sp = hc == 0, hc == NH - 1
            rms_last = p.mm(pb[2 + n][0:1, 0:512], S["rmsones"][:],
                            S["ysq"][:, bass.ts(n, 512)], st, sp,
                            waits=(tq,) + (tuple(prev_tl) if hc == 0 else ()),
                            inc="T" if sp else None)
    tr0 = p.act(S["rmsrow"][0:1, 0:512], pb[2][0:1, 0:512], AF.Sqrt, bias=RMSEPS,
                waits=(rms_last,))
    tr1 = p.act(S["rmsrow"][0:1, 512:1024], pb[3][0:1, 0:512], AF.Sqrt, bias=RMSEPS,
                waits=(rms_last,))
    trc = p.recip(S["rmsrow"][:], S["rmsrow"][:], waits=(tr0, tr1))
    rrep_tok = None
    for n in range(2):
        rrep_tok = p.mm(pb[4 + n][0:128, 0:512], S["onesrowf"][0:1, :],
                        S["rmsrow"][0:1, bass.ts(n, 512)], True, True,
                        waits=(trc,) + tuple(prev_tcx), inc="T")

    mark("P4B")
    fence5 = (rrep_tok, trc, tr1)
    xn2_tok = p.dma("q", S["XN2"][:], xn_stash[:],
                    waits=(xn_stash_tok,) + fence5)
    ringz_free = [None] * 8
    zev_pair = {0: (y_tok,), 2: (tr0, tr1)}
    ynz_tok = None
    for m in range(NH):
        slot = m % 8
        twz = p.dma("g", S["ringz"][:, slot, :, :], din["win_z"][m],
                    waits=(ringz_free[slot],) + fence5)
        b0, b1 = (0, 1) if m % 2 == 0 else (2, 3)
        guards = zev_pair[b0]
        tm = None
        for kc in range(NKC):
            st, sp = kc == 0, kc == NKC - 1
            p.mm(pb[b0][0:128, 0:512], S["ringz"][:, slot, kc, :],
                 S["XN2"][:, kc, 3:515], st, sp,
                 waits=(twz, xn2_tok) + guards if kc == 0 else ())
            tm = p.mm(pb[b1][0:128, 0:512], S["ringz"][:, slot, kc, :],
                      S["XN2"][:, kc, 515:1027], st, sp,
                      inc="T" if sp else None)
        ringz_free[slot] = tm
        tsz0 = p.act(S["szt"][:, 0:512], pb[b0][0:128, 0:512], AF.Silu, waits=(tm,))
        tsz1 = p.act(S["szt"][:, 512:1024], pb[b1][0:128, 0:512], AF.Silu,
                     waits=(tm,))
        pair_ev = None
        for n in range(2):
            tty = p.tt("v", S["tmpy"][:, bass.ts(n, 512)],
                       S["XBC"][:, m, n * 512:(n + 1) * 512],
                       pb[4 + n][0:128, 0:512], ALU.mult,
                       waits=(rrep_tok, y_tok, tsz0, tsz1, tap_hold2))
            ynz_tok = p.tt("v", S["XBC"][:, m, n * 512:(n + 1) * 512],
                           S["tmpy"][:, bass.ts(n, 512)],
                           S["szt"][:, bass.ts(n, 512)], ALU.mult, waits=(tty,))
            pair_ev = ynz_tok
        zev_pair[b0] = (pair_ev,)

    mark("P5")
    fence6 = (ynz_tok, ringz_free[7] or ringz_free[0], tsz1)
    acc2_tok = p.dma("q", S["ACC2"][:], acc_stash[:],
                     waits=(acc_stash_tok,) + fence6)
    wor_free = [None, None]
    WOR = (S["wor0"], S["wor1"])
    op_prev = {0: ynz_tok, 4: ynz_tok}
    out_tok = None
    for kb in range(4):
        slot = kb % 2
        twk = p.dma("g", WOR[slot][:, :, :],
                    din["wout"][kb * 8:(kb + 1) * 8].rearrange("k p d -> p k d"),
                    waits=(wor_free[slot],) + (fence5 if kb == 0 else fence6))
        for t in range(8):
            base = 0 if t % 2 == 0 else 4
            tm = None
            for kc in range(8):
                st, sp = kc == 0, kc == 7
                for n in range(4):
                    tm = p.mm(pb[base + n][0:128, 0:512],
                              S["XBC"][:, kb * 8 + kc, bass.ts(t, 128)],
                              WOR[slot][:, kc, bass.ts(n, 512)], st, sp,
                              waits=(twk, ynz_tok, op_prev[base])
                              if (kc == 0 and n == 0) else (),
                              inc="T" if (sp and n == 3) else None)
            evs = None
            for n in range(4):
                evs = p.tt("v", S["ACC2"][:, t, bass.ts(n, 512)],
                           S["ACC2"][:, t, bass.ts(n, 512)],
                           pb[base + n][0:128, 0:512], ALU.add,
                           waits=(tm, acc2_tok))
            op_prev[base] = evs
            out_tok = evs
        wor_free[slot] = out_tok

    mark("P6")
    ydst = y_out.rearrange("(tl q) d -> q tl d", q=128)
    st_tok = p.dma("q", ydst, S["ACC2"][:], waits=(out_tok,))
    p.wait_only("q", (st_tok,))

    mark("P7")
    if trunc is not None:
        cut = marks[trunc]
        for e in p.ops:
            p.ops[e] = p.ops[e][:cut[e]]

    from contextlib import ExitStack
    with ExitStack() as stack:
        sems = {}
        for nm in (["T", "V", "S", "G"]
                   + [f"GQ{i}" for i in range(Prog.NQSEM)]
                   + [f"SQ{i}" for i in range(Prog.NQSEM)]):
            sems[nm] = stack.enter_context(nc.semaphore("sem" + nm))
        blk = stack.enter_context(nc.Block())

        def runner(ops):
            def f(eng):
                for waits, fn, inc, incn in ops:
                    for (s, v) in waits:
                        eng.wait_ge(sems[s], v)
                    if fn is None:
                        continue
                    ins = fn(eng)
                    if inc:
                        ins.then_inc(sems[inc], incn)
            return f

        blk.tensor(runner(p.ops["t"]))
        blk.vector(runner(p.ops["v"]))
        blk.scalar(runner(p.ops["s"]))
        blk.gpsimd(runner(p.ops["g"]))
        blk.sync(runner(p.ops["q"]))

    return nc


_TAPS = ()


def kernel(**inputs):
    shared, per_core = _pack(inputs)
    nc = build_nc(_TAPS)
    in_maps = []
    for c in range(8):
        m = dict(shared)
        m.update(per_core[c])
        in_maps.append(m)
    res = run_bass_kernel_spmd(nc, in_maps, core_ids=list(range(8)))
    out = np.zeros((2, 4096, DM), np.float32)
    of = out.reshape(2 * 4096, DM)
    for c in range(8):
        b, pos = c // 4, c % 4
        r0 = b * 4096 + pos * T
        of[r0:r0 + T] = res.results[c]["y_out"]
    return out


# version 16
# speedup vs baseline: 1.1867x; 1.0023x over previous
from collections import defaultdict

import numpy as np
import ml_dtypes

import concourse.bass as bass
import concourse.mybir as mybir
from concourse.bass_utils import run_bass_kernel_spmd

F32 = mybir.dt.float32
BF16 = mybir.dt.bfloat16
U8 = mybir.dt.uint8
AF = mybir.ActivationFunctionType
ALU = mybir.AluOpType
AX = mybir.AxisListType
BF = ml_dtypes.bfloat16

DM = 2048
DI = 4096
DS = 64
NH = 32
HD = 128
CH = 256
NKC = 16
T = 1024
TH = T + 3
NCH = 4
LNEPS = 1e-5
RMSEPS = 1.1920929e-07
MASKNEG = -1e30

P0, P1A, P1B, P2, P3, P4A, P4B, P5, P6, P7 = range(10)


def _pack(inputs):
    x = np.asarray(inputs["L0_attn_x"], np.float32)
    w_in = np.asarray(inputs["L0_attn_W_in"], np.float32)
    conv_w = np.asarray(inputs["L0_attn_conv_w"], np.float32)
    conv_b = np.asarray(inputs["L0_attn_conv_b"], np.float32)
    a_log = np.asarray(inputs["L0_attn_A_log"], np.float32)
    dt_bias = np.asarray(inputs["L0_attn_dt_bias"], np.float32)
    d_vec = np.asarray(inputs["L0_attn_D"], np.float32)
    w_out = np.asarray(inputs["L0_attn_W_out"], np.float32)
    gw = np.asarray(inputs["L0_ffn_gate_w"])
    gs = np.asarray(inputs["L0_ffn_gate_scale"], np.float32)
    uw = np.asarray(inputs["L0_ffn_up_w"])
    us = np.asarray(inputs["L0_ffn_up_scale"], np.float32)
    dw = np.asarray(inputs["L0_ffn_down_w"])
    dsc = np.asarray(inputs["L0_ffn_down_scale"], np.float32)

    def lowbyte(a):
        return np.ascontiguousarray(a.view(np.uint8).reshape(*a.shape, 4)[..., 0])

    tiles = np.ascontiguousarray(w_in.T).reshape(NKC, 128, 8352)

    def wtile(c0, c1):
        w = tiles[:, :, c0:c1]
        nt = (c1 - c0) // 128
        w = w.reshape(NKC, 128, nt, 128)
        return np.ascontiguousarray(w.transpose(2, 1, 0, 3))

    win_bc = wtile(DI, DI + 4224)
    win_dt = np.ascontiguousarray(tiles[:, :, 8320:8352].transpose(1, 0, 2))
    win_z = wtile(0, DI)
    woutt = np.ascontiguousarray(w_out.T).reshape(32, 128, DM)

    gq = np.ascontiguousarray(
        np.ascontiguousarray(lowbyte(gw).T).reshape(NKC, 128, 8192).transpose(1, 0, 2))
    uq = np.ascontiguousarray(
        np.ascontiguousarray(lowbyte(uw).T).reshape(NKC, 128, 8192).transpose(1, 0, 2))
    dq = np.ascontiguousarray(
        np.ascontiguousarray(lowbyte(dw).T).reshape(64, 128, DM).transpose(1, 0, 2))

    kk = np.arange(128)
    qq = np.arange(CH)
    mneg = np.zeros((128, 2, CH), np.float32)
    for j in range(2):
        mneg[:, j, :] = np.where(qq[None, :] >= (j * 128 + kk)[:, None], 0.0, MASKNEG)

    mneg = mneg.astype(BF)
    shared = dict(
        win_bc=win_bc, win_dt=win_dt, win_z=win_z, wout=woutt,
        wg=gq, wu=uq, wd=dq,
        gs_col=np.ascontiguousarray(gs.reshape(64, 128).T),
        us_col=np.ascontiguousarray(us.reshape(64, 128).T),
        dsc_row=np.ascontiguousarray(dsc.reshape(1, DM)),
        convw=np.ascontiguousarray(conv_w.reshape(33, 128, 4).transpose(1, 0, 2)),
        convb=np.ascontiguousarray(conv_b.reshape(33, 128).T),
        a_log=np.ascontiguousarray(a_log.reshape(NH, 1)),
        dt_bias=np.ascontiguousarray(dt_bias.reshape(NH, 1)),
        d_rep=np.ascontiguousarray(np.broadcast_to(d_vec.reshape(1, NH), (128, NH))),
        ident_bf=np.eye(128, dtype=BF),
        ident_f=np.eye(128, dtype=np.float32),
        ones_row_f=np.ones((1, 128), np.float32),
        ones_row_bf=np.ones((1, 128), dtype=BF),
        neg128_row=np.full((1, 128), -128.0, dtype=BF),
        ones_col_bf=np.ones((128, 1), dtype=BF),
        rms_ones=np.full((128, 1), 1.0 / DI, dtype=BF),
        mneg=mneg,
        sel4=np.ascontiguousarray(
            np.eye(4, dtype=np.float32)[:, :, None] * np.ones((1, 1, 64), np.float32)),
        sel2=np.ascontiguousarray(
            np.eye(2, dtype=np.float32)[:, :, None] * np.ones((1, 1, 128), np.float32)),
        sel2b=np.ascontiguousarray(
            (np.eye(2)[:, :, None] * np.ones((1, 1, 128))).astype(BF)),
        fconst=np.ascontiguousarray(np.broadcast_to(np.array(
            [LNEPS, RMSEPS, -1.0, 1.0, 1.0 / DM, -128.0], np.float32), (128, 6))),
    )

    xf = x.reshape(2 * 4096, DM)
    per_core = []
    for c in range(8):
        b, pos = c // 4, c % 4
        r0 = b * 4096 + pos * T
        xc = np.zeros((TH, DM), np.float32)
        xc[3:] = xf[r0:r0 + T]
        if pos > 0:
            xc[0:3] = xf[r0 - 3:r0]
        m2t = np.zeros((4, 4), np.float32)
        mbias = np.full((4, 1), MASKNEG, np.float32)
        for j in range(4):
            if j < pos:
                mbias[j, 0] = 0.0
                for l in range(j + 1, pos):
                    m2t[l, j] = 1.0
        per_core.append(dict(x=xc, m2t=m2t, mbias=mbias))
    return shared, per_core


class SbufPlan:
    def __init__(self, base, top):
        self.base, self.top = base, top
        self.items = []

    def add(self, name, nbytes, ph0, ph1):
        nbytes = (nbytes + 31) & ~31
        cands = sorted({self.base} | {o + b for (_, o, b, p0, p1) in self.items
                                      if not (p1 < ph0 or p0 > ph1)})
        for off in cands:
            ok = all(p1 < ph0 or p0 > ph1 or off + nbytes <= o or off >= o + b
                     for (_, o, b, p0, p1) in self.items)
            if ok:
                assert off + nbytes <= self.top, \
                    f"sbuf overflow: {name} {nbytes}B at {off} (top {self.top})"
                self.items.append((name, off, nbytes, ph0, ph1))
                return off
        raise AssertionError("unreachable")


class Prog:
    ENGSEM = dict(t="T", v="V", s="S", g="G")

    NQSEM = 6

    def __init__(self):
        self.ops = {e: [] for e in "tvsgq"}
        self.cnt = defaultdict(int)
        self.mark = defaultdict(int)
        self.qidx = {"g": 0, "q": 0}

    def op(self, eng, fn, waits=(), inc=None, incn=1, hard=()):
        own = self.ENGSEM.get(eng)
        rw = []
        for tk in list(waits) + list(hard):
            if tk is None:
                continue
            sem, val = tk
            if sem == own and tk not in hard:
                continue
            if self.mark[(eng, sem)] >= val:
                continue
            self.mark[(eng, sem)] = val
            rw.append((sem, val))
        tok = None
        if inc:
            self.cnt[inc] += incn
            tok = (inc, self.cnt[inc])
        self.ops[eng].append((tuple(rw), fn, inc, incn))
        return tok

    def dma(self, q, out, in_, waits=()):
        base = "GQ" if q == "g" else "SQ"
        sem = f"{base}{self.qidx[q] % self.NQSEM}"
        self.qidx[q] += 1
        self_wait = (sem, self.cnt[sem]) if self.cnt[sem] else None
        return self.op(q, lambda e, o=out, i=in_: e.dma_start(out=o, in_=i),
                       tuple(waits) + (self_wait,), inc=sem, incn=16)

    def mm(self, out, lhsT, rhs, start, stop, waits=(), inc=None):
        return self.op("t", lambda e, o=out, l=lhsT, r=rhs, a=start, b=stop:
                       e.matmul(o, l, r, start=a, stop=b), waits, inc=inc)

    def transp(self, out, in_, ident, waits=(), inc="T"):
        return self.op("t", lambda e, o=out, i=in_, d=ident: e.transpose(o, i, d),
                       waits, inc=inc)

    def act(self, out, in_, func, bias=0.0, scale=1.0, accum=None, waits=(),
            inc="S", hard=()):
        return self.op("s", lambda e, o=out, i=in_, f=func, b=bias, sc=scale, a=accum:
                       e.activation(o, i, f, bias=b, scale=sc, accum_out=a),
                       waits, inc=inc, hard=hard)

    def tt(self, eng, out, a, b, alu, waits=()):
        return self.op(eng, lambda e, o=out, x=a, y=b, u=alu:
                       e.tensor_tensor(o, x, y, u), waits, inc=self.ENGSEM[eng])

    def ts(self, eng, out, in0, s1, s2, op0, op1=ALU.bypass, waits=(), hard=()):
        return self.op(eng, lambda e, o=out, i=in0, a=s1, b=s2, u=op0, w=op1:
                       e.tensor_scalar(o, i, a, b, u, w), waits,
                       inc=self.ENGSEM[eng], hard=hard)

    def stt(self, eng, out, in0, scalar, in1, op0, op1, waits=(), hard=()):
        return self.op(eng, lambda e, o=out, i=in0, sc=scalar, j=in1, u=op0, w=op1:
                       e.scalar_tensor_tensor(o, i, sc, j, u, w),
                       waits, inc=self.ENGSEM[eng], hard=hard)

    def copy(self, eng, out, in_, waits=()):
        return self.op(eng, lambda e, o=out, i=in_: e.tensor_copy(o, i),
                       waits, inc=self.ENGSEM[eng])

    def rsum(self, out, in_, waits=()):
        return self.op("v", lambda e, o=out, i=in_: e.reduce_sum(o, i, AX.X),
                       waits, inc="V")

    def scan(self, out, d0, d1, init, op0, op1, waits=()):
        return self.op("v", lambda e, o=out, a=d0, b=d1, i=init, u=op0, w=op1:
                       e.tensor_tensor_scan(o, a, b, i, u, w), waits, inc="V")

    def recip(self, out, in_, waits=()):
        return self.op("v", lambda e, o=out, i=in_: e.reciprocal(o, i),
                       waits, inc="V")

    def wait_only(self, eng, waits):
        for tk in waits:
            if tk is None:
                continue
            sem, val = tk
            if self.mark[(eng, sem)] >= val:
                continue
            self.mark[(eng, sem)] = val
            self.ops[eng].append((((sem, val),), None, None, 0))


def build_nc(taps=(), trunc=None):
    tapset = {t[0] for t in taps}
    nc = bass.Bass("TRN2", target_bir_lowering=False, debug=False,
                   detect_race_conditions=False)

    din = {}
    for name, shape, dt in [
        ("x", [TH, DM], F32),
        ("win_bc", [33, 128, NKC, 128], F32), ("win_dt", [128, NKC, 32], F32),
        ("win_z", [32, 128, NKC, 128], F32), ("wout", [32, 128, DM], F32),
        ("wg", [128, NKC, 8192], U8), ("wu", [128, NKC, 8192], U8),
        ("wd", [128, 64, DM], U8),
        ("gs_col", [128, 64], F32), ("us_col", [128, 64], F32),
        ("dsc_row", [1, DM], F32),
        ("convw", [128, 33, 4], F32), ("convb", [128, 33], F32),
        ("a_log", [NH, 1], F32), ("dt_bias", [NH, 1], F32), ("d_rep", [128, NH], F32),
        ("ident_bf", [128, 128], BF16), ("ident_f", [128, 128], F32),
        ("ones_row_f", [1, 128], F32), ("ones_row_bf", [1, 128], BF16),
        ("neg128_row", [1, 128], BF16),
        ("ones_col_bf", [128, 1], BF16), ("rms_ones", [128, 1], BF16),
        ("mneg", [128, 2, CH], BF16), ("m2t", [4, 4], F32), ("mbias", [4, 1], F32),
        ("sel4", [4, 4, 64], F32), ("sel2", [2, 2, 128], F32),
        ("sel2b", [2, 2, 128], BF16), ("fconst", [128, 6], F32),
    ]:
        din[name] = nc.dram_tensor(name, shape, dt, kind="ExternalInput").ap()

    y_out = nc.dram_tensor("y_out", [T, DM], F32, kind="ExternalOutput").ap()
    tout = {}
    for tname, shape, dt in taps:
        tout[tname] = nc.dram_tensor("tap_" + tname, shape, dt,
                                     kind="ExternalOutput").ap()

    xn_stash = nc.dram_tensor("xn_stash", [128, NKC, TH], BF16).ap()
    acc_stash = nc.dram_tensor("acc_stash", [128, 8, DM], F32).ap()
    cc_in = nc.dram_tensor("cc_in", [65, 4096], F32).ap()
    cc_out = nc.dram_tensor("cc_out", [260, 4096], F32).ap()

    plan = SbufPlan((nc.sbuf_base + 31) & ~31, nc.sbuf_top)
    S = {}

    def alloc(name, shape, dt, ph0, ph1):
        per_part = int(np.prod(shape[1:])) * mybir.dt.size(dt)
        off = plan.add(name, per_part, ph0, ph1)
        S[name] = nc.alloc_sbuf_tensor_at(name, list(shape), dt, offset=off)

    specs = [
        ("fconst", [128, 6], F32, P0, P7),
        ("identb", [128, 128], BF16, P0, P7), ("identf", [128, 128], F32, P0, P7),
        ("onesrowf", [1, 128], F32, P0, P7), ("onesrowb", [1, 128], BF16, P0, P7),
        ("sel4", [4, 4, 64], F32, P0, P7), ("sel2", [2, 2, 128], F32, P0, P7),
        ("sel2b", [2, 2, 128], BF16, P0, P7),
        ("neg128", [1, 128], BF16, P0, P7),
        ("onescol", [128, 1], BF16, P0, P7), ("rmsones", [128, 1], BF16, P0, P7),
        ("gscol", [128, 64], F32, P0, P1B), ("uscol", [128, 64], F32, P0, P1B),
        ("drep", [128, NH], F32, P0, P7),
        ("alog", [NH, 1], F32, P0, P7), ("acol", [NH, 1], F32, P0, P7),
        ("dtb", [NH, 1], F32, P0, P7),
        ("m2t", [4, 4], F32, P0, P7), ("mbias", [4, 1], F32, P0, P7),
        ("ldcol", [NH, 1], F32, P4A, P4B),
        ("ldrow", [1, NH], F32, P4A, P4A),
        ("mneg", [128, 2, CH], BF16, P0, P4B),
        ("convw", [128, 33, 4], F32, P0, P3), ("convb", [128, 33], F32, P0, P3),
        ("dscrow", [1, DM], F32, P0, P1A),
        ("screp", [128, DM], BF16, P1A, P1B), ("nsd", [1, DM], BF16, P1A, P1B),
        ("rsxn", [1, T], BF16, P1A, P1B), ("rsh", [1, T], BF16, P1B, P1B),
        ("xhalo", [3, DM], F32, P0, P0),
        ("lnscr", [128, DM], F32, P0, P0), ("lnsq", [128, DM], F32, P0, P0),
        ("lncol", [128, 64], F32, P0, P0), ("xntm", [128, DM], BF16, P0, P0),
        ("XN", [128, NKC, TH], BF16, P0, P2),
        ("XN2", [128, NKC, TH], BF16, P5, P5),
        ("ACC", [128, 8, DM], F32, P0, P1B),
        ("ACC2", [128, 8, DM], F32, P6, P7),
        ("XBC", [128, 33, TH], BF16, P2, P6),
        ("wgu", [128, 2, NKC, 512], BF16, P1B, P1B),
        ("tmpg", [128, 4, T], BF16, P1B, P1B),
        ("hb", [128, 8, T], BF16, P1B, P1B),
        ("wd", [128, 8, DM], BF16, P1B, P1B),
        ("ring", [128, 8, NKC, 128], BF16, P2, P2),
        ("ringz", [128, 8, NKC, 128], BF16, P5, P5),
        ("dt", [NH, T], F32, P2, P4A),
        ("dtT", [128, 8, NH], F32, P4A, P4B),
        ("dA", [NH, T], F32, P4A, P4A),
        ("acs", [NH, T], F32, P4A, P4B),
        ("acsT", [128, 8, NH], F32, P4A, P4B),
        ("dcol", [128, 8, NH], F32, P4A, P4A),
        ("decrep", [64, NCH, NH], F32, P4A, P4B),
        ("decrow", [1, NCH, NH], F32, P4A, P4A),
        ("lastrow", [1, NCH, NH], F32, P4A, P4A),
        ("rrow", [2, 2, 2, 128], F32, P4B, P4B),
        ("exprow", [2, 2, 2, 128], BF16, P4B, P4B),
        ("lbf2", [128, 2, 512], BF16, P4B, P4B),
        ("rhs2", [128, 2, 2, 512], BF16, P4B, P4B),
        ("cs0", [128, T], F32, P2, P4A), ("cs1", [128, T], F32, P2, P4A),
        ("xd", [128, 2, 2, NH, HD], BF16, P4A, P4B),
        ("bt", [128, 2, 2, DS], BF16, P4A, P4A),
        ("bd", [128, 4, DS], BF16, P4A, P4A),
        ("states", [64, NCH, DI], BF16, P4A, P4B),
        ("carA", [64, DI], F32, P4A, P4B), ("carB", [64, DI], F32, P4A, P4B),
        ("sload", [64, 2, 512], F32, P4B, P4B),
        ("cbuf", [64, T], BF16, P3, P4B),
        ("ssb", [128, 2, CH], BF16, P4B, P4B),
        ("ltmp", [128, 2, 512], F32, P4B, P4B),

        ("cx", [64, 2, 2 * CH], BF16, P4B, P4B),
        ("prevbf", [64, 2, 2, HD], BF16, P4B, P4B),
        ("wgt", [4, NH], F32, P4B, P4B), ("wrep", [64, 4, NH], F32, P4B, P4B),
        ("ldg", [4, NH], F32, P4B, P4B),
        ("zrow", [1, 256], F32, P4A, P4A),
        ("ysq", [128, T], BF16, P4B, P4B),
        ("rmsrow", [1, T], F32, P4B, P5),
        ("szt", [128, T], BF16, P5, P5), ("tmpy", [128, T], BF16, P5, P5),
        ("wor0", [128, 8, DM], BF16, P5, P6),
        ("wor1", [128, 8, DM], BF16, P6, P6),
    ]
    specs.sort(key=lambda a: -int(np.prod(a[1][1:])) * mybir.dt.size(a[2]))
    for args in specs:
        alloc(*args)

    pb = [nc.alloc_psum_tensor(f"pb{i}", [128, 512], F32) for i in range(8)]

    def pbf(i):
        return pb[i].ap().bitcast(BF16)

    p = Prog()
    marks = {}

    def mark(name):
        marks[name] = {e: len(p.ops[e]) for e in p.ops}

    src_of = {"identb": "ident_bf", "identf": "ident_f", "onesrowf": "ones_row_f",
              "onesrowb": "ones_row_bf", "sel4": "sel4",
              "neg128": "neg128_row", "onescol": "ones_col_bf", "rmsones": "rms_ones",
              "gscol": "gs_col", "uscol": "us_col", "drep": "d_rep", "alog": "a_log",
              "dtb": "dt_bias", "m2t": "m2t", "mbias": "mbias", "mneg": "mneg",
              "sel2": "sel2", "sel2b": "sel2b",
              "convw": "convw", "convb": "convb", "dscrow": "dsc_row",
              "fconst": "fconst"}
    for n, srcn in src_of.items():
        consts_tok = p.dma("q", S[n][:], din[srcn][:])
    for i, v in enumerate([LNEPS, RMSEPS, -1.0, 1.0, 1.0 / DM, -128.0]):
        nc.const_aps.register(F32, v, S["fconst"][:, i:i + 1])

    xmain = din["x"][3:TH, :].rearrange("(tl q) d -> q tl d", q=128)
    tok_x = p.dma("q", S["ACC"][:], xmain)
    tok_xh = p.dma("q", S["xhalo"][:], din["x"][0:3, :])

    xn_evacs = []
    last_tp = None
    for t in range(9):
        if t < 8:
            xin, npt, w = S["ACC"][:, t, :], 128, (tok_x,)
        else:
            xin, npt, w = S["xhalo"][:], 3, (tok_xh,)
        col = S["lncol"]
        c0 = t * 6
        tv = p.rsum(col[0:npt, c0:c0 + 1], xin, waits=w)
        tv = p.act(col[0:npt, c0 + 1:c0 + 2], col[0:npt, c0:c0 + 1], AF.Copy,
                   scale=-1.0 / DM, waits=(tv,))
        ts1 = p.ts("v", S["lnscr"][0:npt, :], xin, col[0:npt, c0 + 1:c0 + 2],
                   None, ALU.add, waits=(tv,))
        ts2 = p.act(S["lnsq"][0:npt, :], S["lnscr"][0:npt, :], AF.Square,
                    accum=col[0:npt, c0 + 2:c0 + 3], waits=(ts1,))
        ts3 = p.act(col[0:npt, c0 + 3:c0 + 4], col[0:npt, c0 + 2:c0 + 3], AF.Sqrt,
                    scale=1.0 / DM, bias=LNEPS, waits=(ts2,))
        tv2 = p.recip(col[0:npt, c0 + 4:c0 + 5], col[0:npt, c0 + 3:c0 + 4],
                      waits=(ts3,))
        ts4 = p.act(S["xntm"][0:npt, :], S["lnscr"][0:npt, :], AF.Copy,
                    scale=col[0:npt, c0 + 4:c0 + 5],
                    waits=(tv2, consts_tok, last_tp))
        for kc in range(NKC):
            bank = 6 + (kc % 2)
            guard = xn_evacs[-2] if len(xn_evacs) >= 2 else None
            last_tp = p.transp(pbf(bank)[0:128, 0:npt],
                               S["xntm"][0:npt, bass.ts(kc, 128)],
                               S["identb"][0:npt, 0:npt], waits=(ts4, guard))
            dst = (S["XN"][:, kc, 3 + t * 128: 3 + (t + 1) * 128] if t < 8
                   else S["XN"][:, kc, 0:3])
            xn_evacs.append(p.act(dst, pbf(bank)[0:128, 0:npt], AF.Copy,
                                  waits=(last_tp,)))
    xn_tok = xn_evacs[-1]
    if "xn" in tapset:
        p.dma("q", tout["xn"][:], S["XN"][:], waits=(xn_tok,))

    mark("P0")
    sr_tok = None
    for n in range(4):
        tm = p.mm(pb[4][0:128, 0:512], S["onesrowf"][0:1, :],
                  S["dscrow"][0:1, bass.ts(n, 512)], True, True,
                  waits=(consts_tok, sr_tok, xn_tok), inc="T")
        sr_tok = p.copy("v", S["screp"][:, bass.ts(n, 512)], pb[4][0:128, 0:512],
                        waits=(tm,))
    tnsd = p.act(S["nsd"][:], S["dscrow"][:], AF.Copy, scale=-128.0,
                 waits=(consts_tok,))
    rsxn_tok = None
    for n in range(2):
        tm = None
        for kc in range(NKC):
            tm = p.mm(pb[6][0:1, 0:512], S["onescol"][:],
                      S["XN"][:, kc, 3 + n * 512: 3 + (n + 1) * 512],
                      kc == 0, kc == NKC - 1,
                      waits=(xn_tok, consts_tok, rsxn_tok) if kc == 0 else (),
                      inc="T" if kc == NKC - 1 else None)
        rsxn_tok = p.act(S["rsxn"][0:1, bass.ts(n, 512)], pb[6][0:1, 0:512],
                         AF.Copy, waits=(tm,))

    mark("P1A")
    ffn_last = None
    hrow_last = None
    wd_free = None
    hb_free = None
    wgu_last = [None, None]
    dn_prev = None
    wg_pref = None
    bank_guard = {0: None, 2: None}
    for pair in range(8):
        twd = p.dma("g", S["wd"][:], din["wd"][:, pair * 8:(pair + 1) * 8, :],
                    waits=(wd_free, sr_tok, xn_tok))
        wdsc = None
        for ic in range(8):
            wdsc = p.tt("v", S["wd"][:, ic, :], S["wd"][:, ic, :], S["screp"][:],
                        ALU.mult, waits=(twd,))
        gate_ev = {}
        h_tok = None
        up_prev_sb = None
        for sbi in range(2):
            for mat in range(2):
                slot = (sbi * 2 + mat) % 2
                wsrc = din["wg"] if mat == 0 else din["wu"]
                cols = slice((pair * 2 + sbi) * 512, (pair * 2 + sbi + 1) * 512)
                if pair > 0 and sbi == 0 and mat == 0:
                    wt = wg_pref
                else:
                    wt = p.dma("g", S["wgu"][:, slot, :, :], wsrc[:, :, cols],
                               waits=(wgu_last[slot], xn_tok))
                for m in range(4):
                    ic = sbi * 4 + m
                    gic = pair * 8 + ic
                    b0, b1 = (0, 1) if m % 2 == 0 else (2, 3)
                    first_w = [wt, xn_tok, dn_prev, up_prev_sb, bank_guard[b0]]
                    if mat == 1:
                        first_w += [gate_ev[(m, 0)], gate_ev[(m, 1)]]
                    for kc in range(NKC):
                        st = kc == 0
                        p.mm(pb[b0][0:128, 0:512],
                             S["wgu"][:, slot, kc, bass.ts(m, 128)],
                             S["XN"][:, kc, 3:515], st, False,
                             waits=tuple(first_w) if kc == 0 else ())
                        p.mm(pb[b1][0:128, 0:512],
                             S["wgu"][:, slot, kc, bass.ts(m, 128)],
                             S["XN"][:, kc, 515:1027], st, False)
                    p.mm(pb[b0][0:128, 0:512], S["neg128"][:],
                         S["rsxn"][0:1, 0:512], False, True, waits=(rsxn_tok,))
                    tm = p.mm(pb[b1][0:128, 0:512], S["neg128"][:],
                              S["rsxn"][0:1, 512:1024], False, True, inc="T")
                    wgu_last[slot] = tm
                    if mat == 0:
                        gate_ev[(m, 0)] = p.act(
                            S["tmpg"][:, m, 0:512], pb[b0][0:128, 0:512], AF.Silu,
                            scale=S["gscol"][:, gic:gic + 1],
                            waits=(tm, up_prev_sb))
                        gate_ev[(m, 1)] = p.act(
                            S["tmpg"][:, m, 512:1024], pb[b1][0:128, 0:512],
                            AF.Silu, scale=S["gscol"][:, gic:gic + 1], waits=(tm,))
                        bank_guard[b0] = gate_ev[(m, 1)]
                    else:
                        p.stt("v", S["hb"][:, ic, 0:512], pb[b0][0:128, 0:512],
                              S["uscol"][:, gic:gic + 1], S["tmpg"][:, m, 0:512],
                              ALU.mult, ALU.mult, waits=(tm, hb_free))
                        h_tok = p.stt("v", S["hb"][:, ic, 512:1024],
                                      pb[b1][0:128, 0:512],
                                      S["uscol"][:, gic:gic + 1],
                                      S["tmpg"][:, m, 512:1024],
                                      ALU.mult, ALU.mult, waits=(tm,))
                        bank_guard[b0] = h_tok
            up_prev_sb = h_tok
        for n in range(2):
            for ic in range(8):
                st = pair == 0 and ic == 0
                sp = pair == 7 and ic == 7
                tk = p.mm(pb[4 + n][0:1, 0:512], S["onescol"][:],
                          S["hb"][:, ic, bass.ts(n, 512)], st, sp,
                          waits=(h_tok,) if ic == 0 else (),
                          inc="T" if sp else None)
                if sp:
                    hrow_last = tk
        for t in range(8):
            b0, b1 = (0, 1) if t % 2 == 0 else (2, 3)
            for nb in range(2):
                tm = None
                for ic in range(8):
                    st = ic == 0
                    sp = ic == 7
                    p.mm(pb[b0][0:128, 0:512], S["hb"][:, ic, bass.ts(t, 128)],
                         S["wd"][:, ic, nb * 1024: nb * 1024 + 512], st, sp,
                         waits=(wdsc, h_tok, bank_guard[b0]) if ic == 0 else ())
                    tm = p.mm(pb[b1][0:128, 0:512], S["hb"][:, ic, bass.ts(t, 128)],
                              S["wd"][:, ic, nb * 1024 + 512:(nb + 1) * 1024],
                              st, sp, inc="T" if sp else None)
                p.tt("v", S["ACC"][:, t, nb * 1024: nb * 1024 + 512],
                     S["ACC"][:, t, nb * 1024: nb * 1024 + 512],
                     pb[b0][0:128, 0:512], ALU.add, waits=(tm, tok_x))
                ffn_last = p.tt("v", S["ACC"][:, t, nb * 1024 + 512:(nb + 1) * 1024],
                                S["ACC"][:, t, nb * 1024 + 512:(nb + 1) * 1024],
                                pb[b1][0:128, 0:512], ALU.add, waits=(tm,))
                bank_guard[b0] = ffn_last
        dn_prev = ffn_last
        wd_free = ffn_last
        hb_free = ffn_last
        if pair < 7:
            ncols = slice((pair + 1) * 2 * 512, ((pair + 1) * 2 + 1) * 512)
            wg_pref = p.dma("g", S["wgu"][:, 0, :, :], din["wg"][:, :, ncols],
                            waits=(wgu_last[0], xn_tok))

    e0 = p.act(S["rsh"][0:1, 0:512], pb[4][0:1, 0:512], AF.Copy, waits=(hrow_last,))
    e1 = p.act(S["rsh"][0:1, 512:1024], pb[5][0:1, 0:512], AF.Copy,
               waits=(hrow_last,))
    corr_prev = ffn_last
    acc_stash_tok = None
    acc_toks = []
    for t in range(8):
        for n in range(4):
            bank = n % 2
            tm = p.mm(pb[bank][0:128, 0:512], S["rsh"][0:1, bass.ts(t, 128)],
                      S["nsd"][0:1, bass.ts(n, 512)], True, True,
                      waits=(e0, e1, tnsd, corr_prev), inc="T")
            corr_prev = p.tt("v", S["ACC"][:, t, bass.ts(n, 512)],
                             S["ACC"][:, t, bass.ts(n, 512)],
                             pb[bank][0:128, 0:512], ALU.add, waits=(tm,))
        acc_stash_tok = p.dma("q", acc_stash[:, t, :], S["ACC"][:, t, :],
                              waits=(corr_prev,) +
                              (tuple(acc_toks) if t == 7 else ()))
        acc_toks.append(acc_stash_tok)
    ffn_tok = corr_prev
    if "ffn" in tapset:
        p.dma("q", tout["ffn"][:], S["ACC"][:], waits=(ffn_tok,))

    mark("P1B")
    ring_free = [None] * 8
    xbc_evac = []
    ev_pair = {0: None, 2: None}
    evh_prev = None
    dt_tok = None
    conv_done = []
    for mi in range(34):
        m = 33 if mi == 0 else mi - 1
        slot = mi % 8
        wsrc = din["win_bc"][m] if m < 33 else din["win_dt"]
        wdst = S["ring"][:, slot, :, :] if m < 33 else S["ring"][:, slot, :, 0:32]
        tw = p.dma("g", wdst, wsrc, waits=(ring_free[slot], acc_stash_tok))
        b0, b1 = (0, 1) if mi % 2 == 0 else (2, 3)
        mw = 128 if m < 33 else 32
        tm = None
        for kc in range(NKC):
            st, sp = kc == 0, kc == NKC - 1
            lhs = S["ring"][:, slot, kc, 0:mw]
            p.mm(pb[b0][0:mw, 0:512], lhs, S["XN"][:, kc, 3:515], st, sp,
                 waits=(tw, xn_tok, ev_pair[b0]) if kc == 0 else ())
            tm = p.mm(pb[b1][0:mw, 0:512], lhs, S["XN"][:, kc, 515:1027], st, sp,
                      inc="T" if sp else None)
        if m < 33:
            th = None
            for kc in range(NKC):
                th = p.mm(pb[6][0:mw, 0:3], S["ring"][:, slot, kc, 0:mw],
                          S["XN"][:, kc, 0:3], kc == 0, kc == NKC - 1,
                          waits=(evh_prev,) if kc == 0 else (),
                          inc="T" if kc == NKC - 1 else None)
            ring_free[slot] = th
            ev0 = p.act(S["XBC"][:, m, 3:515], pb[b0][0:128, 0:512], AF.Copy,
                        waits=(tm,))
            ev1 = p.act(S["XBC"][:, m, 515:1027], pb[b1][0:128, 0:512], AF.Copy,
                        waits=(tm,))
            evh_prev = p.act(S["XBC"][:, m, 0:3], pb[6][0:128, 0:3], AF.Copy,
                             waits=(th,))
            ev_pair[b0] = ev1
            xbc_evac.append((ev0, ev1, evh_prev))
            w = (ev0, ev1, evh_prev) + ((conv_done[-1],) if conv_done else ())
            t0c = p.ts("v", S["cs0"][:], S["XBC"][:, m, 0:T], S["convw"][:, m, 0:1],
                       S["convb"][:, m:m + 1], ALU.mult, ALU.add, waits=w)
            t1c = p.stt("v", S["cs1"][:], S["XBC"][:, m, 1:T + 1],
                        S["convw"][:, m, 1:2], S["cs0"][:], ALU.mult, ALU.add,
                        waits=(t0c,))
            t2c = p.stt("v", S["cs0"][:], S["XBC"][:, m, 2:T + 2],
                        S["convw"][:, m, 2:3], S["cs1"][:], ALU.mult, ALU.add,
                        waits=(t1c,))
            t3c = p.stt("v", S["cs1"][:], S["XBC"][:, m, 3:T + 3],
                        S["convw"][:, m, 3:4], S["cs0"][:], ALU.mult, ALU.add,
                        waits=(t2c,))
            conv_done.append(p.act(S["XBC"][:, m, 0:T], S["cs1"][:], AF.Silu,
                                   waits=(t3c,)))
        else:
            ring_free[slot] = tm
            last_a = None
            for half, bank in ((0, b0), (1, b1)):
                cols = slice(half * 512, (half + 1) * 512)
                r = p.act(S["cs0"][0:NH, cols], pb[bank][0:NH, 0:512], AF.Relu,
                          bias=S["dtb"][:], waits=(tm,))
                a = p.act(S["cs1"][0:NH, cols], pb[bank][0:NH, 0:512], AF.Abs,
                          bias=S["dtb"][:], waits=(tm,))
                last_a = a
                e = p.act(S["cs1"][0:NH, cols], S["cs1"][0:NH, cols], AF.Exp,
                          scale=-1.0, waits=(a,))
                l = p.act(S["cs1"][0:NH, cols], S["cs1"][0:NH, cols], AF.Ln,
                          bias=1.0, waits=(e,))
                dt_tok = p.tt("v", S["dt"][0:NH, cols], S["cs0"][0:NH, cols],
                              S["cs1"][0:NH, cols], ALU.add, waits=(l, r))
            ev_pair[b0] = last_a
    xn_stash_tok = p.dma("q", xn_stash[:], S["XN"][:], waits=(dt_tok, tm))
    tap_hold = None
    if "xbc" in tapset:
        tap_hold = p.dma("q", tout["xbc"][:], S["XBC"][:],
                         waits=(xbc_evac[-1][0], xbc_evac[-1][1]))

    conv_tok = conv_done[-1]
    if "xc" in tapset:
        p.dma("q", tout["xc"][:], S["XBC"][:, :, 0:T], waits=(conv_tok,))
    cb_tok = p.copy("v", S["cbuf"][:], S["XBC"][64:128, 32, 0:T], waits=(conv_tok,))

    mark("P3")
    tae = p.act(S["acol"][:], S["alog"][:], AF.Exp, waits=(consts_tok,))
    tac = p.act(S["acol"][:], S["acol"][:], AF.Copy, scale=-1.0, waits=(tae,))
    tda = p.ts("v", S["dA"][:], S["dt"][:], S["acol"][:], None, ALU.mult,
               waits=(tac, dt_tok, xn_stash_tok, acc_stash_tok))
    acs_tok = None
    for c in range(NCH):
        cols = slice(c * CH, (c + 1) * CH)
        acs_tok = p.scan(S["acs"][:, cols], S["dA"][:, cols], S["dA"][:, cols],
                         0.0, ALU.add, ALU.bypass, waits=(tda,))
    ld_tok = None
    tp_prev2 = [None, None]
    tp_toks = []
    for grp, srcbuf, dstbuf, wtok in ((0, "dt", "dtT", dt_tok),
                                      (1, "acs", "acsT", acs_tok)):
        for i in range(8):
            bank = 6 + i % 2
            tt = p.transp(pb[bank][0:128, 0:NH], S[srcbuf][:, bass.ts(i, 128)],
                          S["identf"][0:NH, 0:NH],
                          waits=(wtok, tp_prev2[i % 2], xn_stash_tok,
                                 acc_stash_tok))
            if dstbuf == "dtT":
                tp_prev2[i % 2] = p.copy("v", S[dstbuf][:, i, :],
                                         pb[bank][0:128, 0:NH], waits=(tt,))
            else:
                tp_prev2[i % 2] = p.act(S[dstbuf][:, i, :], pb[bank][0:128, 0:NH],
                                        AF.Copy, waits=(tt,))
            tp_toks.append(tp_prev2[i % 2])
    acst_tok = tp_toks[-1]
    dtT_tok = tp_toks[7]

    dcol_tok = None
    dec_tok = None
    lr_prev = None
    dr_prev = None
    last_ev_prev = acst_tok
    for c in range(NCH):
        tlt = p.transp(pb[3][0:1, 0:NH],
                       S["acs"][:, (c + 1) * CH - 1:(c + 1) * CH],
                       S["identf"][0:NH, 0:NH],
                       waits=(acs_tok, last_ev_prev, dr_prev))
        tle = p.act(S["lastrow"][0:1, c, :], pb[3][0:1, 0:NH], AF.Copy,
                    waits=(tlt,))
        last_ev_prev = tle
        tm = p.mm(pb[2][0:128, 0:NH], S["onesrowf"][0:1, :],
                  S["lastrow"][0:1, c, :], True, True,
                  waits=(tle, lr_prev), inc="T")
        for j in range(2):
            buf = "cs0" if j == 0 else "cs1"
            tv = p.tt("v", S[buf][0:128, 0:NH], pb[2][0:128, 0:NH],
                      S["acsT"][:, 2 * c + j, :], ALU.subtract,
                      waits=(tm, acst_tok))
            dcol_tok = p.act(S["dcol"][:, 2 * c + j, :], S[buf][0:128, 0:NH],
                             AF.Exp, waits=(tv,))
            lr_prev = tv
        trr = p.act(S["decrow"][0:1, c, :], S["lastrow"][0:1, c, :],
                    AF.Exp, waits=(tle,))
        tm2 = p.mm(pb[3][0:64, 0:NH], S["onesrowf"][0:1, 0:64],
                   S["decrow"][0:1, c, :], True, True,
                   waits=(trr, dr_prev, lr_prev), inc="T")
        dec_tok = p.act(S["decrep"][:, c, :], pb[3][0:64, 0:NH], AF.Copy,
                        waits=(tm2,))
        dr_prev = dec_tok

    ld_tok = p.tt("v", S["ldrow"][:], S["lastrow"][0:1, 0, :],
                  S["lastrow"][0:1, 1, :], ALU.add, waits=(last_ev_prev,))
    ld_tok = p.tt("v", S["ldrow"][:], S["ldrow"][:], S["lastrow"][0:1, 2, :],
                  ALU.add, waits=(ld_tok,))
    ld_tok = p.tt("v", S["ldrow"][:], S["ldrow"][:], S["lastrow"][0:1, 3, :],
                  ALU.add, waits=(ld_tok,))

    stash_fence = (xn_stash_tok, acc_stash_tok)
    xd_state = {"last_evac": acst_tok}

    def emit_xd(c, buf, extra=()):
        toks = []
        for j in range(2):
            for h in range(NH):
                bank = 6 + (h % 2)
                guard = toks[-2] if len(toks) >= 2 else xd_state["last_evac"]
                tt = p.transp(pbf(bank)[0:128, 0:128],
                              S["XBC"][:, h, c * CH + j * 128: c * CH + (j + 1) * 128],
                              S["identb"][:],
                              waits=(conv_tok, guard) + stash_fence + tuple(extra))
                toks.append(p.act(S["xd"][:, buf, j, h, :], pbf(bank)[0:128, 0:128],
                                  AF.Copy, scale=S["dtT"][:, 2 * c + j, h:h + 1],
                                  waits=(tt, dtT_tok)))
        xd_state["last_evac"] = toks[-1]
        return toks[-1]

    def emit_bt(c, buf):
        toks = []
        for j in range(2):
            bank = 6 + j
            tt = p.transp(pbf(bank)[0:128, 0:64],
                          S["XBC"][0:64, 32, c * CH + j * 128: c * CH + (j + 1) * 128],
                          S["identb"][0:64, 0:64],
                          waits=(conv_tok, xd_state["last_evac"],
                                 toks[-1] if toks else None))
            toks.append(p.act(S["bt"][:, buf, j, :], pbf(bank)[0:128, 0:64],
                              AF.Copy, waits=(tt,)))
        xd_state["last_evac"] = toks[-1]
        return toks[-1]

    mark("P4A_T")
    scan1_tok = None
    st_ev = None
    prev_tmm = None
    st_bank_rd = [dec_tok, dec_tok]
    bd_guard = [None] * 4
    for c in range(NCH):
        mark(f"P4A_C{c}")
        buf = c % 2
        xd_tok = emit_xd(c, buf)
        if c == 0:
            mark("P4A_XD")
            if "xd0" in tapset:
                tk = p.dma("q", tout["xd0"][:], S["xd"][:, 0, :, :, :],
                           waits=(xd_tok,))
                xd_state["last_evac"] = tk
        bt_tok = emit_bt(c, buf)
        if c == 0:
            mark("P4A_BT")
            if "bt0" in tapset:
                tk = p.dma("q", tout["bt0"][:], S["bt"][:, 0, :, :],
                           waits=(bt_tok,))
                xd_state["last_evac"] = tk
        for h in range(NH):
            if c == 0 and h in (1, 2, 16):
                mark(f"P4A_H{h}")
            bank = h % 2
            tmm = None
            for j in range(2):
                sl = (h % 2) * 2 + j
                tbd = p.ts("v", S["bd"][:, sl, :], S["bt"][:, buf, j, :],
                           S["dcol"][:, 2 * c + j, h:h + 1], None, ALU.mult,
                           waits=(bt_tok, dcol_tok, bd_guard[sl]))
                tmm = p.mm(pb[bank][0:64, 0:128], S["bd"][:, sl, :],
                           S["xd"][:, buf, j, h, :], j == 0, j == 1,
                           waits=(tbd, xd_tok, st_bank_rd[bank]), inc="T")
                bd_guard[sl] = tmm
            prev_tmm = tmm
            st_ev = p.copy("v", S["states"][:, c, h * HD:(h + 1) * HD],
                           pb[bank][0:64, 0:128], waits=(tmm,))
            st_bank_rd[bank] = st_ev
            if c == 0:
                scan1_tok = p.copy("v", S["carA"][:, h * HD:(h + 1) * HD],
                                   pb[bank][0:64, 0:128], waits=(tmm,))
                st_bank_rd[bank] = scan1_tok
        if c > 0:
            prevb = "carA" if c % 2 == 1 else "carB"
            nxtb = "carB" if c % 2 == 1 else "carA"
            for h in range(NH):
                scan1_tok = p.stt("v", S[nxtb][:, h * HD:(h + 1) * HD],
                                  S[prevb][:, h * HD:(h + 1) * HD],
                                  S["decrep"][:, c, h:h + 1],
                                  S["states"][:, c, h * HD:(h + 1) * HD],
                                  ALU.mult, ALU.add, waits=(st_ev, dec_tok))
    if "dt" in tapset:
        p.dma("q", tout["dt"][:], S["dt"][:], waits=(dt_tok,))
    if "ldc" in tapset:
        p.dma("q", tout["ldc"][:], S["ldrow"][:], waits=(ld_tok,))
    if "dcol" in tapset:
        p.dma("q", tout["dcol"][:], S["dcol"][:], waits=(dcol_tok,))
    if "lrow" in tapset:
        p.dma("q", tout["lrow"][:], S["lastrow"][:], waits=(dcol_tok,))
    if "acs" in tapset:
        p.dma("q", tout["acs"][:], S["acs"][:], waits=(acs_tok,))
    if "states" in tapset:
        p.dma("q", tout["states"][:], S["states"][:], waits=(scan1_tok,))
    if "score" in tapset:
        p.dma("q", tout["score"][:], S["carB"][:], waits=(scan1_tok,))
    mark("P4A_PRE")
    tz = p.op("v", lambda e, a=S["zrow"][:]: e.memset(a, 0.0),
              (xn_stash_tok, acc_stash_tok), inc="V")
    tz2 = p.copy("v", S["zrow"][0:1, 0:NH], S["ldrow"][:], waits=(tz, ld_tok))
    t1 = p.dma("q", cc_in[0:64, :], S["carB"][:], waits=(scan1_tok,))
    t2 = None
    for seg in range(16):
        t2 = p.dma("q", cc_in[64:65, 256 * seg:256 * (seg + 1)],
                   S["zrow"][:], waits=(tz2,))
    cc_tok = p.op("g", lambda g: g.collective_compute(
        "AllGather", ALU.bypass, replica_groups=[[0, 1, 2, 3], [4, 5, 6, 7]],
        ins=[cc_in.opt()], outs=[cc_out.opt()]), waits=(t1, t2), inc="G")

    mark("P4A")
    ldg_toks = [p.dma("q", S["ldg"][j:j + 1, :],
                      cc_out[65 * j + 64:65 * j + 65, 0:NH], waits=(cc_tok,))
                for j in range(4)]
    tm = p.mm(pb[2][0:4, 0:NH], S["m2t"][:], S["ldg"][:], True, True,
              waits=tuple(ldg_toks) + (consts_tok, lr_prev), inc="T")
    tw = p.act(S["wgt"][:], pb[2][0:4, 0:NH], AF.Exp, bias=S["mbias"][:],
               waits=(tm,))
    wr_tok = None
    for j in range(4):
        tm = p.mm(pb[3][0:64, 0:NH], S["sel4"][:, j, :], S["wgt"][:],
                  True, True, waits=(tw, dec_tok if j == 0 else wr_tok), inc="T")
        wr_tok = p.act(S["wrep"][:, j, :], pb[3][0:64, 0:NH], AF.Copy,
                       waits=(tm,))
    if "wgt0" in tapset:
        wr_tok = p.dma("q", tout["wgt0"][:], S["wgt"][:], waits=(tw, wr_tok))
        wr_tok = p.dma("q", tout["ldg0"][:], S["ldg"][:], waits=(wr_tok,))
        wr_tok = p.dma("q", tout["wrp0"][:], S["wrep"][:], waits=(wr_tok,))
    comb_tok = None
    comb_half = [None] * 4
    sload_rd = [None, None]
    nld = 0
    for half in range(4):
        for j in range(4):
            for q2 in range(2):
                sl = nld % 2
                nld += 1
                tload = p.dma("q", S["sload"][:, sl, :],
                              cc_out[65 * j:65 * j + 64,
                                     half * 1024 + q2 * 512:
                                     half * 1024 + (q2 + 1) * 512],
                              waits=(cc_tok, sload_rd[sl]))
                for hh in range(4):
                    h = half * 8 + q2 * 4 + hh
                    if j == 0:
                        comb_tok = p.ts("v", S["carA"][:, h * HD:(h + 1) * HD],
                                        S["sload"][:, sl, hh * HD:(hh + 1) * HD],
                                        S["wrep"][:, 0, h:h + 1], None, ALU.mult,
                                        waits=(tload, scan1_tok, wr_tok))
                    else:
                        src = "carA" if j % 2 == 1 else "carB"
                        dst = "carB" if j % 2 == 1 else "carA"
                        comb_tok = p.stt("v", S[dst][:, h * HD:(h + 1) * HD],
                                         S["sload"][:, sl, hh * HD:(hh + 1) * HD],
                                         S["wrep"][:, j, h:h + 1],
                                         S[src][:, h * HD:(h + 1) * HD],
                                         ALU.mult, ALU.add,
                                         waits=(tload, wr_tok))
                sload_rd[sl] = comb_tok
        comb_half[half] = comb_tok
    car = "carB"
    if "cin" in tapset:
        tap_cin = p.dma("q", tout["cin"][:], S["carB"][:], waits=(comb_tok,))
        comb_tok = tap_cin

    y_tok = None
    s_prev = None
    prev_ymm = [None, None]
    prev_tl = [lr_prev, lr_prev]
    prev_te = [None, None]
    prev_rhs = [None, None]
    prev_tcx = [wr_tok, wr_tok]
    pv_guard = [[None, None], [None, None]]
    prev_tce = [None, None]
    prev_tr = None
    rrow_guard = None
    y_bank_prev = [scan1_tok, scan1_tok]
    for c in range(NCH):
        buf = c % 2
        xd_tok = emit_xd(c, buf)
        rrow_guard = rrow_guard or xd_tok
        s_toks = []
        for j in range(2):
            tmS = p.mm(pb[j][0:128, 0:CH],
                       S["XBC"][0:64, 32, c * CH + j * 128: c * CH + (j + 1) * 128],
                       S["cbuf"][:, c * CH:(c + 1) * CH], True, True,
                       waits=(conv_tok, cb_tok, s_prev,
                              y_bank_prev[j]), inc="T")
            s_toks.append(p.act(S["ssb"][:, j, :], pb[j][0:128, 0:CH], AF.Copy,
                                waits=(tmS,)))
            s_prev = s_toks[-1]
            y_bank_prev[j] = s_toks[-1]
        for hp in range(NH // 2):
            par = hp % 2
            h0 = hp * 2
            rev = None
            for j in range(2):
                bank = 6 + j
                ttr = p.transp(pb[bank][0:2, 0:128],
                               S["acsT"][:, 2 * c + j, h0:h0 + 2], S["identf"][:],
                               waits=(acst_tok, rrow_guard, xd_tok))
                rev = p.act(S["rrow"][0:2, par, j, :], pb[bank][0:2, 0:128],
                            AF.Copy, waits=(ttr, prev_tr))
            rrow_guard = rev
            tex = p.act(S["exprow"][0:2, par, :, :], S["rrow"][0:2, par, :, :],
                        AF.Exp, waits=(rev, prev_tce[par]))
            p.mm(pb[2 + par][0:128, 0:CH], S["sel2"][:, 0, :],
                 S["rrow"][0:2, par, :, :].rearrange("p a b -> p (a b)"),
                 True, False, waits=(rev, prev_tl[par]))
            tr = p.mm(pb[2 + par][0:128, CH:2 * CH], S["sel2"][:, 1, :],
                      S["rrow"][0:2, par, :, :].rearrange("p a b -> p (a b)"),
                      False, True, inc="T")
            prev_tr = tr
            p.mm(pb[4 + par][0:64, 0:CH], S["sel2b"][:, 0, 0:64],
                 S["exprow"][0:2, par, :, :].rearrange("p a b -> p (a b)"),
                 True, False, waits=(tex, prev_tcx[par]))
            tce = p.mm(pb[4 + par][0:64, CH:2 * CH], S["sel2b"][:, 1, 0:64],
                       S["exprow"][0:2, par, :, :].rearrange("p a b -> p (a b)"),
                       False, True, inc="T")
            prev_tce[par] = tce
            tcx = p.tt("v", S["cx"][:, par, :].rearrange("p (a b) -> p a b", a=2),
                       S["cbuf"][:, c * CH:(c + 1) * CH].rearrange(
                           "p (a b) -> p a b", a=1).to_broadcast([64, 2, CH]),
                       pb[4 + par][0:64, 0:2 * CH].rearrange(
                           "p (a b) -> p a b", a=2),
                       ALU.mult, waits=(tce, cb_tok, prev_ymm[par]))
            prev_tcx[par] = tcx
            for j in range(2):
                tla = p.stt("v", S["ltmp"][:, par, 0:CH], pb[2 + par][0:128, 0:CH],
                            S["acsT"][:, 2 * c + j, h0:h0 + 1], S["mneg"][:, j, :],
                            ALU.subtract, ALU.add,
                            waits=(tr, acst_tok, consts_tok, prev_te[par]))
                tlb = p.stt("v", S["ltmp"][:, par, CH:2 * CH],
                            pb[2 + par][0:128, CH:2 * CH],
                            S["acsT"][:, 2 * c + j, h0 + 1:h0 + 2],
                            S["mneg"][:, j, :], ALU.subtract, ALU.add,
                            waits=(tla,))
                prev_tl[par] = tlb
                te = p.act(S["lbf2"][:, par, :], S["ltmp"][:, par, :], AF.Exp,
                           waits=(tlb, prev_rhs[par]))
                prev_te[par] = te
                trh = p.tt("v", S["rhs2"][:, par, j, :].rearrange(
                               "p (a b) -> p a b", a=2),
                           S["ssb"][:, j, :].rearrange(
                               "p (a b) -> p a b", a=1).to_broadcast([128, 2, CH]),
                           S["lbf2"][:, par, :].rearrange("p (a b) -> p a b", a=2),
                           ALU.mult, waits=(te, s_toks[j], prev_ymm[par]))
                prev_rhs[par] = trh
            for dh in range(2):
                h = h0 + dh
                tpv = p.copy("v", S["prevbf"][:, par, dh, :],
                             S[car][:, h * HD:(h + 1) * HD],
                             waits=(comb_half[h // 8], pv_guard[par][dh]))
                p.mm(pb[par][0:128, 0:CH], S["xd"][:, buf, 0, h, :],
                     S["rhs2"][:, par, 0, dh * CH:(dh + 1) * CH], True, False,
                     waits=(prev_rhs[par], xd_tok, y_bank_prev[par]))
                p.mm(pb[par][0:128, 0:CH], S["xd"][:, buf, 1, h, :],
                     S["rhs2"][:, par, 1, dh * CH:(dh + 1) * CH], False, False)
                tmy = p.mm(pb[par][0:128, 0:CH], S["prevbf"][:, par, dh, :],
                           S["cx"][:, par, dh * CH:(dh + 1) * CH],
                           False, True, waits=(tpv, tcx), inc="T")
                prev_ymm[par] = tmy
                pv_guard[par][dh] = tmy
                ysl = S["XBC"][:, h, c * CH:(c + 1) * CH]
                y_tok = p.stt("v", ysl, ysl, S["drep"][:, h:h + 1],
                              pb[par][0:128, 0:CH], ALU.mult, ALU.add,
                              waits=(tmy, consts_tok))
                y_bank_prev[par] = y_tok
        xd_state["last_evac"] = rrow_guard
        if c < NCH - 1:
            nxt = "carA" if car == "carB" else "carB"
            for h in range(NH):
                comb_tok = p.stt("v", S[nxt][:, h * HD:(h + 1) * HD],
                                 S[car][:, h * HD:(h + 1) * HD],
                                 S["decrep"][:, c, h:h + 1],
                                 S["states"][:, c, h * HD:(h + 1) * HD],
                                 ALU.mult, ALU.add, waits=(y_tok, dec_tok))
            car = nxt
    tap_hold2 = None
    if "yssd" in tapset:
        tap_hold2 = p.dma("q", tout["yssd"][:], S["XBC"][:, 0:32, 0:T],
                          waits=(y_tok,))

    rms_last = None
    tq = None
    for hc in range(NH):
        tq = p.tt("v", S["ysq"][:], S["XBC"][:, hc, 0:T], S["XBC"][:, hc, 0:T],
                  ALU.mult, waits=(y_tok, rms_last) if hc == 0 else (y_tok,))
        for n in range(2):
            st, sp = hc == 0, hc == NH - 1
            rms_last = p.mm(pb[2 + n][0:1, 0:512], S["rmsones"][:],
                            S["ysq"][:, bass.ts(n, 512)], st, sp,
                            waits=(tq,) + (tuple(prev_tl) if hc == 0 else ()),
                            inc="T" if sp else None)
    tr0 = p.act(S["rmsrow"][0:1, 0:512], pb[2][0:1, 0:512], AF.Sqrt, bias=RMSEPS,
                waits=(rms_last,))
    tr1 = p.act(S["rmsrow"][0:1, 512:1024], pb[3][0:1, 0:512], AF.Sqrt, bias=RMSEPS,
                waits=(rms_last,))
    trc = p.recip(S["rmsrow"][:], S["rmsrow"][:], waits=(tr0, tr1))
    rrep_tok = None
    for n in range(2):
        rrep_tok = p.mm(pb[4 + n][0:128, 0:512], S["onesrowf"][0:1, :],
                        S["rmsrow"][0:1, bass.ts(n, 512)], True, True,
                        waits=(trc,) + tuple(prev_tcx), inc="T")

    mark("P4B")
    fence5 = (rrep_tok, trc, tr1)
    xn2_tok = p.dma("q", S["XN2"][:], xn_stash[:],
                    waits=(xn_stash_tok,) + fence5)
    ringz_free = [None] * 8
    zev_pair = {0: (y_tok,), 2: (tr0, tr1)}
    ynz_tok = None
    for m in range(NH):
        slot = m % 8
        twz = p.dma("g", S["ringz"][:, slot, :, :], din["win_z"][m],
                    waits=(ringz_free[slot],) + fence5)
        b0, b1 = (0, 1) if m % 2 == 0 else (2, 3)
        guards = zev_pair[b0]
        tm = None
        for kc in range(NKC):
            st, sp = kc == 0, kc == NKC - 1
            p.mm(pb[b0][0:128, 0:512], S["ringz"][:, slot, kc, :],
                 S["XN2"][:, kc, 3:515], st, sp,
                 waits=(twz, xn2_tok) + guards if kc == 0 else ())
            tm = p.mm(pb[b1][0:128, 0:512], S["ringz"][:, slot, kc, :],
                      S["XN2"][:, kc, 515:1027], st, sp,
                      inc="T" if sp else None)
        ringz_free[slot] = tm
        tsz0 = p.act(S["szt"][:, 0:512], pb[b0][0:128, 0:512], AF.Silu, waits=(tm,))
        tsz1 = p.act(S["szt"][:, 512:1024], pb[b1][0:128, 0:512], AF.Silu,
                     waits=(tm,))
        pair_ev = None
        for n in range(2):
            tty = p.tt("v", S["tmpy"][:, bass.ts(n, 512)],
                       S["XBC"][:, m, n * 512:(n + 1) * 512],
                       pb[4 + n][0:128, 0:512], ALU.mult,
                       waits=(rrep_tok, y_tok, tsz0, tsz1, tap_hold2))
            ynz_tok = p.tt("v", S["XBC"][:, m, n * 512:(n + 1) * 512],
                           S["tmpy"][:, bass.ts(n, 512)],
                           S["szt"][:, bass.ts(n, 512)], ALU.mult, waits=(tty,))
            pair_ev = ynz_tok
        zev_pair[b0] = (pair_ev,)

    mark("P5")
    fence6 = (ynz_tok, ringz_free[7] or ringz_free[0], tsz1)
    acc2_tok = p.dma("q", S["ACC2"][:], acc_stash[:],
                     waits=(acc_stash_tok,) + fence6)
    wor_free = [None, None]
    WOR = (S["wor0"], S["wor1"])
    op_prev = {0: ynz_tok, 4: ynz_tok}
    out_tok = None
    for kb in range(4):
        slot = kb % 2
        twk = p.dma("g", WOR[slot][:, :, :],
                    din["wout"][kb * 8:(kb + 1) * 8].rearrange("k p d -> p k d"),
                    waits=(wor_free[slot],) + (fence5 if kb == 0 else fence6))
        for t in range(8):
            base = 0 if t % 2 == 0 else 4
            tm = None
            for kc in range(8):
                st, sp = kc == 0, kc == 7
                for n in range(4):
                    tm = p.mm(pb[base + n][0:128, 0:512],
                              S["XBC"][:, kb * 8 + kc, bass.ts(t, 128)],
                              WOR[slot][:, kc, bass.ts(n, 512)], st, sp,
                              waits=(twk, ynz_tok, op_prev[base])
                              if (kc == 0 and n == 0) else (),
                              inc="T" if (sp and n == 3) else None)
            evs = None
            for n in range(4):
                evs = p.tt("v", S["ACC2"][:, t, bass.ts(n, 512)],
                           S["ACC2"][:, t, bass.ts(n, 512)],
                           pb[base + n][0:128, 0:512], ALU.add,
                           waits=(tm, acc2_tok))
            op_prev[base] = evs
            out_tok = evs
        wor_free[slot] = out_tok

    mark("P6")
    ydst = y_out.rearrange("(tl q) d -> q tl d", q=128)
    st_tok = p.dma("q", ydst, S["ACC2"][:], waits=(out_tok,))
    p.wait_only("q", (st_tok,))

    mark("P7")
    if trunc is not None:
        cut = marks[trunc]
        for e in p.ops:
            p.ops[e] = p.ops[e][:cut[e]]

    from contextlib import ExitStack
    with ExitStack() as stack:
        sems = {}
        for nm in (["T", "V", "S", "G"]
                   + [f"GQ{i}" for i in range(Prog.NQSEM)]
                   + [f"SQ{i}" for i in range(Prog.NQSEM)]):
            sems[nm] = stack.enter_context(nc.semaphore("sem" + nm))
        blk = stack.enter_context(nc.Block())

        def runner(ops):
            def f(eng):
                for waits, fn, inc, incn in ops:
                    for (s, v) in waits:
                        eng.wait_ge(sems[s], v)
                    if fn is None:
                        continue
                    ins = fn(eng)
                    if inc:
                        ins.then_inc(sems[inc], incn)
            return f

        blk.tensor(runner(p.ops["t"]))
        blk.vector(runner(p.ops["v"]))
        blk.scalar(runner(p.ops["s"]))
        blk.gpsimd(runner(p.ops["g"]))
        blk.sync(runner(p.ops["q"]))

    return nc


_TAPS = ()


def kernel(**inputs):
    shared, per_core = _pack(inputs)
    nc = build_nc(_TAPS)
    in_maps = []
    for c in range(8):
        m = dict(shared)
        m.update(per_core[c])
        in_maps.append(m)
    res = run_bass_kernel_spmd(nc, in_maps, core_ids=list(range(8)))
    out = np.zeros((2, 4096, DM), np.float32)
    of = out.reshape(2 * 4096, DM)
    for c in range(8):
        b, pos = c // 4, c % 4
        r0 = b * 4096 + pos * T
        of[r0:r0 + T] = res.results[c]["y_out"]
    return out
